# version 35
# baseline (speedup 1.0000x reference)
import contextlib
import numpy as np

import concourse.bass as bass
import concourse.mybir as mybir
from concourse.bass_utils import run_bass_kernel_spmd

F32 = mybir.dt.float32
BF16 = mybir.dt.bfloat16
AX = mybir.AxisListType
ALU = mybir.AluOpType
ACT = mybir.ActivationFunctionType

B = 8
D = 1024
NBLK = 8
EPS = 1e-5
CLIP = 88.0
H0 = slice(0, 512)
H1 = slice(512, 1024)


def _nm(ap):
    n = ap.name
    return n() if callable(n) else n


class Bld:

    def __init__(self):
        self.SCH = {k: [] for k in ("t", "v", "s", "y", "g")}
        self.cnt = {}
        self.dirty = {k: set() for k in self.SCH}
        self.wm = {}

    def op(self, k, f, ins=(), outs=(), inc=None, amt=1):
        nm = {_nm(a) for a in ins} | {_nm(a) for a in outs}
        if k != "t" and (nm & self.dirty[k]):
            self.SCH[k].append(lambda E: E.drain())
            self.dirty[k].clear()
        if inc is not None:
            self.cnt[inc] = self.cnt.get(inc, 0) + amt
            self.SCH[k].append(lambda E, f=f, inc=inc, amt=amt: f(E).then_inc(inc, amt))
            ret = self.cnt[inc]
        else:
            self.SCH[k].append(lambda E, f=f: f(E))
            ret = None
        self.dirty[k] |= {_nm(a) for a in outs}
        return ret

    def wait(self, k, sem, val):
        if not val:
            return
        assert val <= self.cnt.get(sem, 0), f"wait beyond scheduled ({val} > {self.cnt.get(sem, 0)})"
        wm = self.wm.setdefault(k, {})
        if wm.get(sem, -1) >= val:
            return
        wm[sem] = val
        self.SCH[k].append(lambda E, sem=sem, val=val: E.wait_ge(sem, val))

    def inc(self, k, sem, amt=1):
        self.cnt[sem] = self.cnt.get(sem, 0) + amt
        self.SCH[k].append(lambda E, sem=sem, amt=amt: E.drain().then_inc(sem, amt))
        self.dirty[k].clear()
        return self.cnt[sem]


def build_nc(n_tiles):
    nc = bass.Bass("TRN2", target_bir_lowering=False, debug=False)
    S_loc = n_tiles * 128
    assert n_tiles % 2 == 0 and n_tiles >= 4

    x_hbm = nc.dram_tensor("x", [S_loc, D], F32, kind="ExternalInput").ap()
    w_hbm = {}
    for name, shp in (("wq", [NBLK, 128, 128]), ("wk", [NBLK, 128, 128]),
                      ("wv", [NBLK, 128, 128]), ("ad", [NBLK, 128, 128]),
                      ("bd", [NBLK, 128, 128]), ("cd", [NBLK, 128, 128]),
                      ("wo", [D, D]), ("gamma", [D]), ("beta", [D]),
                      ("f1", [D, D]), ("f2", [D, D]), ("f3", [D, D]), ("fb", [D]),
                      ("eye", [128, 128]), ("onehot", [1, B])):
        w_hbm[name] = nc.dram_tensor(name, shp, F32, kind="ExternalInput").ap()
    out_hbm = nc.dram_tensor("out", [S_loc, D], F32, kind="ExternalOutput").ap()
    cc_in = nc.dram_tensor("cc_in", [1, B], F32)
    cc_out = nc.dram_tensor("cc_out", [1, B], F32)
    scr2 = nc.dram_tensor("scr2", [1, 2], F32)
    scrb = nc.dram_tensor("scrb", [1, D], F32)

    es = contextlib.ExitStack()
    sb = lambda n, s, d=F32: es.enter_context(nc.sbuf_tensor(n, s, d))

    gh = sb("gh", [128, NBLK, 387], BF16)
    gl = sb("gl", [128, NBLK, 387], BF16)
    woh = sb("woh", [128, NBLK, D], BF16)
    wol = sb("wol", [128, NBLK, D], BF16)
    fwh = [sb(f"f{j}h", [128, NBLK, D], BF16) for j in (1, 2, 3)]
    fwl = [sb(f"f{j}l", [128, NBLK, D], BF16) for j in (1, 2, 3)]
    bao_rep = sb("bao_rep", [128, D], F32)
    fb_rep = sb("fb_rep", [128, D], F32)
    bao_row = sb("bao_row", [1, D], F32)
    ones_row = sb("ones_row", [1, 128], BF16)
    eye32 = sb("eye32", [128, 128], F32)
    gm = sb("gm", [128, NBLK], F32)
    bm32 = sb("bm32", [128, NBLK], F32)
    bmh = sb("bmh", [128, NBLK], BF16)
    bml = sb("bml", [128, NBLK], BF16)

    xbuf = sb("xbuf", [128, 2, D], F32)
    xn = sb("xn", [128, D], F32)
    wstage = sb("wstage", [128, D], F32)
    xnh_fm = sb("xnh_fm", [128, D], BF16)
    xnl_fm = sb("xnl_fm", [128, D], BF16)
    u0 = sb("u0", [128, D], F32)
    u4k_0 = sb("u4k_0", [128, 2048], BF16)
    u4k_1 = sb("u4k_1", [128, 2048], BF16)
    u4k_2 = sb("u4k_2", [128, 2048], BF16)
    outbuf = sb("outbuf", [128, D], F32)
    stats6 = sb("stats6", [128, 2, 2, 6], F32)
    mv = sb("mv", [128, 2, 2], F32)
    negmu = sb("negmu", [128, 2], F32)
    la = sb("la", [128, 2], F32)
    rstd = sb("rstd", [128, 2], F32)
    nmr = sb("nmr", [128, 2], F32)
    epst = sb("epst", [128, 1], F32)
    nhc = sb("nhc", [128, 1], F32)
    g3 = sb("g3", [128, 3], F32)
    c01b = sb("c01b", [128, 1], F32)
    omcb = sb("omcb", [128, 1], F32)
    sca = sb("sca", [1, 32], F32)
    oh_sb = sb("oh_sb", [1, B], F32)
    ccv = sb("ccv", [1, B], F32)
    alldwc = sb("alldwc", [1, B], F32)

    stg32b = sb("stg32b", [128, 771], F32)
    stg32b2 = sb("stg32b2", [128, D], F32)
    stg32_p = [wstage[:, 0:771], stg32b[:, :]]
    stgh_p = [u4k_1[:, 0:771], u4k_2[:, 0:771]]
    stgl_p = [u4k_1[:, 1024:1795], u4k_2[:, 1024:1795]]
    wraw_h_p = [u4k_2[:, 0:1024], u4k_1[:, 0:1024]]
    wraw_l_p = [u4k_2[:, 1024:2048], u4k_1[:, 1024:2048]]
    p1set = None
    nfm_h = u4k_1[:, 0:1024]
    nfm_l = u4k_1[:, 1024:2048]
    h1fm_h = u4k_2[:, 0:1024]
    h1fm_l = u4k_2[:, 1024:2048]
    h2fm_h = u4k_0[:, 0:1024]
    h2fm_l = u4k_0[:, 1024:2048]
    m_view = u0[:, :].rearrange("p (n d) -> p n d", n=NBLK)
    p1set = [(u4k_0[:, 0:1024], u4k_0[:, 1024:2048]),
             (xnh_fm[:, :], xnl_fm[:, :])]

    ps_trF = es.enter_context(nc.psum_tensor("ps_trF", [128, 1024], F32))
    ps_ssm = [es.enter_context(nc.psum_tensor(f"ps_ssm{i}", [128, 512], F32)) for i in range(2)]
    ps_out = es.enter_context(nc.psum_tensor("ps_out", [128, 1024], F32))
    ps_h = es.enter_context(nc.psum_tensor("ps_h", [128, 1024], F32))

    s_x = es.enter_context(nc.semaphore("s_x"))
    s_w = es.enter_context(nc.semaphore("s_w"))
    s_o = es.enter_context(nc.semaphore("s_o"))
    s_c = es.enter_context(nc.semaphore("s_c"))
    s_cc = es.enter_context(nc.semaphore("s_cc"))
    s_t = es.enter_context(nc.semaphore("s_t"))
    s_v = es.enter_context(nc.semaphore("s_v"))
    s_s = es.enter_context(nc.semaphore("s_s"))
    s_g = es.enter_context(nc.semaphore("s_g"))

    b = Bld()
    op, wait, inc = b.op, b.wait, b.inc

    def dma(k, dst, src, sem, outs=None, slow=False, serialize=True):
        if serialize:
            wait(k, sem, b.cnt.get(sem, 0))
        return op(k, lambda E, dst=dst, src=src, slow=slow:
                  E.dma_start(out=dst, in_=src, allow_slow_non_contiguous=slow),
                  outs=outs if outs is not None else [dst], inc=sem, amt=16)

    def mm(pdst, lhsT, rhs, start, stop, inc=None):
        return op("t", lambda E, p=pdst, a=lhsT, bb=rhs, s1=start, s2=stop:
                  E.matmul(p, a, bb, start=s1, stop=s2), inc=inc)

    def mm3(pdst, ah, al, bh, bl, start, stop, inc=None):
        mm(pdst, ah, bh, start, False)
        mm(pdst, ah, bl, False, False)
        return mm(pdst, al, bh, False, stop, inc=inc)

    def split_s(hi_dst, src, ins=()):
        op("s", lambda E, hi_dst=hi_dst, src=src: E.copy(hi_dst, src),
           ins=list(ins) + [src], outs=[hi_dst])

    def split_v(lo_dst, src, hi, ins=()):
        op("v", lambda E, lo_dst=lo_dst, src=src, hi=hi:
           E.scalar_tensor_tensor(lo_dst, src, 1.0, hi, ALU.mult, ALU.subtract),
           ins=list(ins) + [src, hi], outs=[lo_dst])

    def ln_stats(src_ap, p=0):
        for c2 in range(2):
            op("v", lambda E, c2=c2, src_ap=src_ap, p=p:
               E.bn_stats(stats6[:, p, c2, :], src_ap[:, c2 * 512:(c2 + 1) * 512]),
               ins=[src_ap], outs=[stats6])
        op("v", lambda E, p=p: E.bn_aggr(mv[:, p, :],
                                         stats6[:, p, :, :].rearrange("p c s -> p (c s)")),
           ins=[stats6], outs=[mv])
        op("v", lambda E, p=p: E.tensor_scalar(negmu[:, p:p + 1], mv[:, p, 0:1], -1.0,
                                               None, ALU.mult), ins=[mv], outs=[negmu])
        ev = inc("v", s_v)
        wait("s", s_v, ev)
        op("s", lambda E, p=p: E.activation(la[:, p:p + 1], mv[:, p, 1:2], ACT.Ln,
                                            bias=epst[:, 0:1]), ins=[mv, epst], outs=[la])
        op("s", lambda E, p=p: E.activation(rstd[:, p:p + 1], la[:, p:p + 1], ACT.Exp,
                                            scale=nhc[:, 0:1]), ins=[la, nhc], outs=[rstd])
        ev_s = inc("s", s_s)
        wait("v", s_s, ev_s)

    slot_free = {}

    def slot_wait(k, p):
        for sem, val in slot_free.get(_nm(p[:, 0:1]), []):
            wait(k, sem, val)

    def tr_phase(src32_ap, hi_dst, lo_dst, src_wait=(), src_wait2=(), hi_eng="s"):
        slot_wait("t", ps_trF)
        for sem, val in src_wait:
            wait("t", sem, val)
        halves = []
        for half in range(2):
            if half == 1:
                for sem, val in src_wait2:
                    wait("t", sem, val)
            ev_t = None
            for c in range(4 * half, 4 * half + 4):
                ev_t = op("t", lambda E, c=c, src32_ap=src32_ap:
                          E.transpose(ps_trF[:, c * 128:(c + 1) * 128],
                                      src32_ap[:, c * 128:(c + 1) * 128], eye32[:, :]),
                          inc=s_t if c % 4 == 3 else None)
            hs = slice(half * 512, (half + 1) * 512)
            if hi_eng == "s":
                wait("s", s_t, ev_t)
                split_s(hi_dst[:, hs], ps_trF[:, hs])
                se = inc("s", s_s)
                hi_ev = (s_s, se)
            else:
                wait("v", s_t, ev_t)
                op("v", lambda E, hs=hs, hi_dst=hi_dst:
                   E.tensor_copy(hi_dst[:, hs], ps_trF[:, hs]), outs=[hi_dst])
                hi_ev = (s_v, inc("v", s_v))
            wait("v", s_t, ev_t)
            wait("v", hi_ev[0], hi_ev[1])
            split_v(lo_dst[:, hs], ps_trF[:, hs], hi_dst[:, hs])
            ve = inc("v", s_v)
            halves.append(([hi_ev], [(s_v, ve)]))
        slot_free["trF:h0"] = halves[0][0] + halves[0][1]
        slot_free["trF:h1"] = halves[1][0] + halves[1][1]
        slot_free[_nm(ps_trF[:, 0:1])] = (halves[0][0] + halves[0][1]
                                          + halves[1][0] + halves[1][1])
        return halves

    dma("y", eye32[:, :], w_hbm["eye"][:, :], s_w)
    dma("y", gm[:, :], w_hbm["gamma"].rearrange("(c d) -> d c", d=128), s_w, slow=True)
    dma("y", bm32[:, :], w_hbm["beta"].rearrange("(c d) -> d c", d=128), s_w, slow=True)
    dma("y", fb_rep[:, :], w_hbm["fb"].unsqueeze(0).partition_broadcast(128), s_w)
    dma("y", oh_sb[:, :], w_hbm["onehot"][:, :], s_w)
    ev_w0 = b.cnt[s_w]

    for f in (lambda E: E.memset(epst[:, :], EPS),
              lambda E: E.memset(nhc[:, :], -0.5),
              lambda E: E.memset(ones_row[:, :], 1.0),
              lambda E: E.memset(sca[:, :], 0.0)):
        op("v", f, outs=[epst, nhc, ones_row, sca])
    for stg in stg32_p:
        for col in (256, 513, 770):
            op("v", lambda E, stg=stg, col=col: E.memset(stg[:, col:col + 1], 1.0),
               ins=[stg], outs=[stg])
    ev_ones0 = inc("v", s_v)
    op("v", lambda E: E.memset(sca[0:1, 0:1], float(S_loc) + 1.0), ins=[sca], outs=[sca])

    ev_s0 = inc("s", s_s)
    wait("v", s_w, ev_w0)
    ev_v0 = inc("v", s_v)

    arena = [xn, wstage]
    p1arena = [xn, outbuf]

    ev_p1_split = [None] * n_tiles
    ev_p1_xrd = [None] * n_tiles
    ev_p1_gram = [None] * n_tiles

    def p1_load(i):
        if i >= 2:
            wait("y", s_s, ev_p1_xrd[i - 2])
        dma("y", xbuf[:, i % 2, :], x_hbm[i * 128:(i + 1) * 128, :], s_x, outs=[xbuf])

    def p1_preamble(i):
        p = i % 2
        wait("v", s_x, 16 * (i + 1))
        for c2 in range(2):
            op("v", lambda E, c2=c2, i=i, p=p:
               E.bn_stats(stats6[:, p, c2, :], xbuf[:, i % 2, c2 * 512:(c2 + 1) * 512]),
               ins=[xbuf], outs=[stats6])
        op("v", lambda E, p=p: E.bn_aggr(mv[:, p, :],
                                         stats6[:, p, :, :].rearrange("p c s -> p (c s)")),
           ins=[stats6], outs=[mv])
        op("v", lambda E, p=p: E.tensor_scalar(negmu[:, p:p + 1], mv[:, p, 0:1], -1.0,
                                               None, ALU.mult), ins=[mv], outs=[negmu])
        ev_v = inc("v", s_v)
        wait("s", s_v, ev_v)
        op("s", lambda E, p=p: E.activation(la[:, p:p + 1], mv[:, p, 1:2], ACT.Ln,
                                            bias=epst[:, 0:1]), ins=[mv, epst], outs=[la])
        op("s", lambda E, p=p: E.activation(rstd[:, p:p + 1], la[:, p:p + 1], ACT.Exp,
                                            scale=nhc[:, 0:1]), ins=[la, nhc], outs=[rstd])
        op("s", lambda E, p=p: E.activation(nmr[:, p:p + 1], negmu[:, p:p + 1],
                                            ACT.Identity, scale=rstd[:, p:p + 1]),
           ins=[negmu, rstd], outs=[nmr])
        ar = p1arena[i % 2]
        if i >= 2:
            wait("s", s_t, ev_p1_gram[i - 2])
        op("s", lambda E, i=i, ar=ar, p=p: E.activation(ar[:, :], xbuf[:, i % 2, :],
                                                        ACT.Identity,
                                                        bias=nmr[:, p:p + 1],
                                                        scale=rstd[:, p:p + 1]),
           ins=[xbuf, nmr, rstd], outs=[ar])
        ev_xn = inc("s", s_s)
        ev_p1_xrd[i] = ev_xn
        ev_p1_split[i] = ev_xn

    def p1_gram(i):
        ar = p1arena[i % 2]
        wait("t", s_s, ev_p1_split[i])
        for n in range(NBLK):
            csl = slice(n * 128, (n + 1) * 128)
            first = (i == 0) and (n in (0, 4))
            last = (i == n_tiles - 1) and (n in (3, 7))
            ev_p1_gram[i] = mm(ps_out[:, csl], ar[:, csl], ar[:, csl], first, last,
                               inc=s_t if n == NBLK - 1 else None)

    TK = {"split": (ev_s0, ev_v0), "evac_s": None, "evac_v": None, "gmm": None,
          "bias_t": None, "wo": None, "f": None, "bao": None, "gmm_final": None}

    def g_task(n):
        stg32 = stg32_p[n % 2]
        wait("y", s_v, ev_ones0)
        if TK["gmm"] is not None:
            wait("y", s_t, TK["gmm"])
        wait("y", s_w, b.cnt.get(s_w, 0))
        for dst0, srcn in ((0, "wq"), (128, "ad"), (257, "wk"), (385, "bd"),
                           (514, "wv"), (642, "cd")):
            dma("y", stg32[:, dst0:dst0 + 128], w_hbm[srcn][n], s_w, outs=[stg32],
                serialize=False)
        ev_ld = b.cnt[s_w]
        return lambda: g_back(n, ev_ld)

    def g_back(n, ev_ld):
        stg32 = stg32_p[n % 2]
        wait("t", s_w, ev_ld)
        wait("t", s_v, ev_ones0)
        if TK["evac_s"] is not None:
            wait("t", s_s, TK["evac_s"])
            wait("t", s_v, TK["evac_v"])
        mm(ps_ssm[0][:, 0:257], stg32[:, 0:128], stg32[:, 128:385], True, True)
        mm(ps_ssm[1][:, 0:129], stg32[:, 257:385], stg32[:, 385:514], True, True)
        ev_gmm = TK["gmm"] = mm(ps_ssm[1][:, 129:258], stg32[:, 514:642],
                                stg32[:, 642:771], True, True, inc=s_t)
        evmap = ((0, ps_ssm[0][:, 0:128]), (384, ps_ssm[0][:, 128:129]),
                 (128, ps_ssm[1][:, 0:128]), (385, ps_ssm[1][:, 128:129]),
                 (256, ps_ssm[1][:, 129:257]), (386, ps_ssm[1][:, 257:258]))
        wait("s", s_t, ev_gmm)
        for dc, pslc in evmap:
            w_ = pslc.shape[1]
            op("s", lambda E, n=n, dc=dc, p=pslc, w_=w_: E.copy(gh[:, n, dc:dc + w_], p),
               outs=[gh])
        TK["evac_s"] = inc("s", s_s)
        wait("v", s_t, ev_gmm)
        wait("v", s_s, TK["evac_s"])
        for dc, pslc in evmap:
            w_ = pslc.shape[1]
            op("v", lambda E, n=n, dc=dc, p=pslc, w_=w_:
               E.scalar_tensor_tensor(gl[:, n, dc:dc + w_], p, 1.0, gh[:, n, dc:dc + w_],
                                      ALU.mult, ALU.subtract), ins=[gh], outs=[gl])
        op("v", lambda E, n=n: E.tensor_copy(m_view[:, n, :], ps_ssm[0][:, 129:257]),
           outs=[u0])
        TK["evac_v"] = inc("v", s_v)
        return None

    def wo_task(c):
        if TK["wo"] is None:
            TK["wo"] = (None, TK["split"][1])
            TK["gmm_final"] = TK["gmm"]
        wait("y", s_v, TK["wo"][1])
        wait("y", s_t, TK["gmm_final"])
        dma("y", wstage[:, :], w_hbm["wo"][c * 128:(c + 1) * 128, :], s_w)
        if TK["bias_t"] is not None:
            wait("y", s_t, TK["bias_t"])
        else:
            wait("y", s_t, TK["gmm_final"])
        dma("y", stg32b2[:, :], w_hbm["wo"][c * 128:(c + 1) * 128, :], s_w)
        ev_ld = b.cnt[s_w]
        wait("v", s_w, ev_ld)
        op("v", lambda E, c=c: E.tensor_scalar(wstage[:, :], wstage[:, :], gm[:, c:c + 1],
                                               None, ALU.mult), ins=[wstage, gm], outs=[wstage])
        ev_sc = inc("v", s_v)
        wait("s", s_v, ev_sc)
        op("s", lambda E, c=c: E.copy(woh[:, c, :], wstage[:, :]), ins=[wstage], outs=[woh])
        ev_wh = inc("s", s_s)
        wait("v", s_s, ev_wh)
        op("v", lambda E, c=c: E.scalar_tensor_tensor(wol[:, c, :], wstage[:, :], 1.0,
                                                      woh[:, c, :], ALU.mult, ALU.subtract),
           ins=[wstage, woh], outs=[wol])
        TK["wo"] = (ev_wh, inc("v", s_v))
        return lambda: wo_back(c, ev_ld)

    def wo_back(c, ev_ld):
        wait("t", s_w, ev_ld)
        mm(ps_h[0:1, H0], bm32[:, c:c + 1], stg32b2[:, H0],
           start=(c == 0), stop=(c == NBLK - 1))
        TK["bias_t"] = mm(ps_h[0:1, H1], bm32[:, c:c + 1], stg32b2[:, H1],
                          start=(c == 0), stop=(c == NBLK - 1), inc=s_t)
        return None

    def bias_evac_task():
        TK["f"] = TK["wo"]

        def back():
            wait("v", s_t, TK["bias_t"])
            op("v", lambda E: E.tensor_copy(bao_row[:, :], ps_h[0:1, :]), outs=[bao_row])
            ev_bs = inc("v", s_v)
            wait("g", s_v, ev_bs)
            dma("g", scrb[:, :].opt(), bao_row[:, :], s_c, outs=[])
            dma("g", bao_rep[:, :], scrb.ap()[0:1, :].partition_broadcast(128), s_c)
            TK["bao"] = b.cnt[s_c]
        return back

    def ffn_task(j, c):
        wait("y", s_s, TK["f"][0])
        wait("y", s_v, TK["f"][1])
        dma("y", wstage[:, :], w_hbm[f"f{j + 1}"][c * 128:(c + 1) * 128, :], s_w)
        ev_ld = b.cnt[s_w]
        wait("s", s_w, ev_ld)
        split_s(fwh[j][:, c, :], wstage[:, :])
        ev_fh = inc("s", s_s)

        def back():
            wait("v", s_w, ev_ld)
            wait("v", s_s, ev_fh)
            split_v(fwl[j][:, c, :], wstage[:, :], fwh[j][:, c, :])
            TK["f"] = (ev_fh, inc("v", s_v))
        return back

    TASKS = ([(lambda n=n: g_task(n)) for n in range(NBLK)]
             + [(lambda c=c: wo_task(c)) for c in range(NBLK)]
             + [bias_evac_task]
             + [(lambda j=j, c=c: ffn_task(j, c)) for j in range(3) for c in range(NBLK)])
    MAXLEAD = [1, 1] + [2] * (NBLK - 2) + [1] * NBLK + [2] + [1] * (3 * NBLK)

    p1_load(0)
    p1_load(1)
    p1_preamble(0)
    p1_preamble(1)
    _ntask = 0
    _nback = 0
    _backs = []
    for i in range(n_tiles):
        p1_gram(i)
        if i + 2 < n_tiles:
            p1_load(i + 2)
            p1_preamble(i + 2)
        tgt_b = i * len(TASKS) // n_tiles
        tgt_f = (i + 1) * len(TASKS) // n_tiles
        while _nback < min(_ntask, tgt_b):
            f = _backs[_nback]
            if f is not None:
                f()
            _nback += 1
        while _ntask < tgt_f:
            while _nback < _ntask + 1 - MAXLEAD[_ntask]:
                f = _backs[_nback]
                if f is not None:
                    f()
                _nback += 1
            _backs.append(TASKS[_ntask]())
            _ntask += 1
    while _nback < len(TASKS):
        f = _backs[_nback]
        if f is not None:
            f()
        _nback += 1



    xfh = xnh_fm[:, :].rearrange("p (n d) -> p n d", n=NBLK)
    xfl = xnl_fm[:, :].rearrange("p (n d) -> p n d", n=NBLK)
    nfh = nfm_h.rearrange("p (n d) -> p n d", n=NBLK)
    nfl = nfm_l.rearrange("p (n d) -> p n d", n=NBLK)
    h1h = h1fm_h.rearrange("p (n d) -> p n d", n=NBLK)
    h1l = h1fm_l.rearrange("p (n d) -> p n d", n=NBLK)
    h2h = h2fm_h.rearrange("p (n d) -> p n d", n=NBLK)
    h2l = h2fm_l.rearrange("p (n d) -> p n d", n=NBLK)

    st = {"psh_rd": None,
          "psh_rd0": None,
          "out_rd": None,
          "store": [0] * (n_tiles + 1),
          "xfree": [None] * (n_tiles + 2)}
    ev_p2_xn = [None] * (n_tiles + 1)

    def p2_load(i):
        if i >= n_tiles:
            return
        if i >= 2:
            wait("y", s_v, st["xfree"][i - 2])
        else:
            wait("y", s_s, ev_p1_xrd[n_tiles - 2 + i])
        dma("y", xbuf[:, i % 2, :], x_hbm[i * 128:(i + 1) * 128, :], s_x, outs=[xbuf])

    def p2_preamble(i):
        if i >= n_tiles:
            return
        wait("v", s_x, 16 * (n_tiles + i + 1))
        ln_stats(xbuf[:, i % 2, :])
        ar = arena[i % 2]
        op("v", lambda E, i=i, ar=ar: E.tensor_scalar(ar[:, :], xbuf[:, i % 2, :],
                                                      negmu[:, 0:1], rstd[:, 0:1],
                                                      ALU.add, ALU.mult),
           ins=[xbuf, negmu, rstd], outs=[ar])
        ev_p2_xn[i] = inc("v", s_v)

    ssm_slots = [(ps_ssm[0], 0, None), (ps_ssm[1], 0, None),
                 (ps_trF, 0, "trF:h0"), (ps_trF, 512, "trF:h1")]

    def grp_ssm_ffn1(i, c, spl):
        bank, off, skey = ssm_slots[c % 4]
        for sem, val in spl[0]:
            wait("t", sem, val)
        if skey:
            for sem, val in slot_free.get(skey, []):
                wait("t", sem, val)
        else:
            slot_wait("t", bank)
        Xh, Xl = xfh[:, c, :], xfl[:, c, :]
        bk = bank[:, off:off + 387]
        mm(bk, Xh, gh[:, c, :], True, False)
        mm(bk, Xh, gl[:, c, :], False, False)
        for sem, val in spl[1]:
            wait("t", sem, val)
        ev_ssm = mm(bk, Xl, gh[:, c, :], False, True, inc=s_t)
        if c == 0:
            if st["psh_rd0"]:
                wait("t", s_v, st["psh_rd0"])
            mm(ps_h[:, H0], Xh, fwh[0][:, c, H0], True, False)
            mm(ps_h[:, H0], Xh, fwl[0][:, c, H0], False, False)
            mm(ps_h[:, H0], Xl, fwh[0][:, c, H0], False, False)
            if st["psh_rd"]:
                wait("t", s_v, st["psh_rd"])
            mm(ps_h[:, H1], Xh, fwh[0][:, c, H1], True, False)
            mm(ps_h[:, H1], Xh, fwl[0][:, c, H1], False, False)
            ev_grp = mm(ps_h[:, H1], Xl, fwh[0][:, c, H1], False, False, inc=s_t)
            return ev_ssm, None, ev_grp
        if c == NBLK - 1:
            mm(ps_h[:, H0], Xh, fwh[0][:, c, H0], False, False)
            mm(ps_h[:, H0], Xh, fwl[0][:, c, H0], False, False)
            ev_h0 = mm(ps_h[:, H0], Xl, fwh[0][:, c, H0], False, True, inc=s_t)
            mm(ps_h[:, H1], Xh, fwh[0][:, c, H1], False, False)
            mm(ps_h[:, H1], Xh, fwl[0][:, c, H1], False, False)
            ev_grp = mm(ps_h[:, H1], Xl, fwh[0][:, c, H1], False, True, inc=s_t)
            return ev_ssm, ev_h0, ev_grp
        mm(ps_h[:, H0], Xh, fwh[0][:, c, H0], c == 0, False)
        mm(ps_h[:, H1], Xh, fwh[0][:, c, H1], c == 0, False)
        mm(ps_h[:, H0], Xh, fwl[0][:, c, H0], False, False)
        mm(ps_h[:, H1], Xh, fwl[0][:, c, H1], False, False)
        mm(ps_h[:, H0], Xl, fwh[0][:, c, H0], False, False)
        ev_grp = mm(ps_h[:, H1], Xl, fwh[0][:, c, H1], False, False, inc=s_t)
        return ev_ssm, None, ev_grp

    def blend(i, c, ev_ssm):
        bank, off, skey = ssm_slots[c % 4]
        ar = arena[i % 2]
        wait("v", s_t, ev_ssm)
        if i == 0 and c == 0:
            wait("v", s_c, ev_bc)
        op("v", lambda E, bank=bank, off=off: E.tensor_scalar(
            g3[:, :], bank[:, off + 384:off + 387], c01b[:, 0:1], omcb[:, 0:1],
            ALU.mult, ALU.add), ins=[c01b, omcb], outs=[g3])
        for t3 in range(3):
            op("v", lambda E, bank=bank, off=off, t3=t3, c=c, ar=ar:
               E.scalar_tensor_tensor(ar[:, c * 128:(c + 1) * 128],
                                      bank[:, off + t3 * 128:off + (t3 + 1) * 128],
                                      g3[:, t3:t3 + 1], ar[:, c * 128:(c + 1) * 128],
                                      ALU.mult, ALU.add),
               ins=[g3, ar], outs=[ar])
        ev = inc("v", s_v)
        slot_free[_nm(bank[:, 0:1])] = [(s_v, ev)]
        if skey:
            slot_free[skey] = [(s_v, ev)]
        return ev

    def wide_grp(stat_h, stat_l, c, wh, wl, pdst, start, stop, hi_wait=(), lo_wait=(),
                 inc_last=None, h0_first=False):
        for sem, val in hi_wait:
            wait("t", sem, val)
        Sh, Sl = stat_h[:, c, :], stat_l[:, c, :]
        if h0_first:
            mm(pdst[:, H0], Sh, wh[:, c, H0], start, False)
            mm(pdst[:, H0], Sh, wl[:, c, H0], False, False)
            for sem, val in lo_wait:
                wait("t", sem, val)
            ev_h0 = mm(pdst[:, H0], Sl, wh[:, c, H0], False, stop, inc=inc_last)
            mm(pdst[:, H1], Sh, wh[:, c, H1], start, False)
            mm(pdst[:, H1], Sh, wl[:, c, H1], False, False)
            return ev_h0, mm(pdst[:, H1], Sl, wh[:, c, H1], False, stop, inc=inc_last)
        mm(pdst[:, H0], Sh, wh[:, c, H0], start, False)
        mm(pdst[:, H1], Sh, wh[:, c, H1], start, False)
        mm(pdst[:, H0], Sh, wl[:, c, H0], False, False)
        mm(pdst[:, H1], Sh, wl[:, c, H1], False, False)
        for sem, val in lo_wait:
            wait("t", sem, val)
        ev_h0 = mm(pdst[:, H0], Sl, wh[:, c, H0], False, stop, inc=inc_last)
        return ev_h0, mm(pdst[:, H1], Sl, wh[:, c, H1], False, stop, inc=inc_last)


    wait("v", s_t, ev_p1_gram[n_tiles - 1])
    op("v", lambda E: E.tensor_tensor(outbuf[:, :], ps_out[:, :], u0[:, :], ALU.mult),
       ins=[u0, outbuf], outs=[outbuf])
    op("v", lambda E: E.reduce_sum(negmu[:, 0:1], outbuf[:, :], axis=AX.X),
       ins=[outbuf], outs=[negmu])
    ev_red = inc("v", s_v)
    wait("t", s_v, ev_red)
    op("t", lambda E: E.transpose(ps_trF[0:1, 0:128], negmu[:, 0:1], eye32[:, :]))
    ev_tt = inc("t", s_t)
    wait("v", s_t, ev_tt)
    op("v", lambda E: E.reduce_sum(sca[0:1, 1:2], ps_trF[0:1, 0:128], axis=AX.X),
       ins=[sca], outs=[sca])
    op("v", lambda E: E.tensor_scalar(sca[0:1, 2:3], sca[0:1, 1:2], 1.0 / float(S_loc),
                                      None, ALU.mult), ins=[sca], outs=[sca])
    ev_v1 = inc("v", s_v)
    wait("s", s_v, ev_v1)
    op("s", lambda E: E.activation(sca[0:1, 3:4], sca[0:1, 0:1], ACT.Ln), ins=[sca], outs=[sca])
    ev_s1 = inc("s", s_s)
    wait("v", s_s, ev_s1)
    op("v", lambda E: E.reciprocal(sca[0:1, 4:5], sca[0:1, 3:4]), ins=[sca], outs=[sca])
    op("v", lambda E: E.tensor_scalar(sca[0:1, 5:6], sca[0:1, 2:3], 10.0, None, ALU.mult),
       ins=[sca], outs=[sca])
    op("v", lambda E: E.tensor_tensor(sca[0:1, 6:7], sca[0:1, 5:6], sca[0:1, 4:5], ALU.mult),
       ins=[sca], outs=[sca])
    op("v", lambda E: E.tensor_scalar(sca[0:1, 7:8], sca[0:1, 6:7], -1.0, CLIP,
                                      ALU.mult, ALU.min), ins=[sca], outs=[sca])
    op("v", lambda E: E.tensor_scalar(sca[0:1, 7:8], sca[0:1, 7:8], -CLIP, None, ALU.max),
       ins=[sca], outs=[sca])
    ev_v2 = inc("v", s_v)
    wait("s", s_v, ev_v2)
    op("s", lambda E: E.activation(sca[0:1, 8:9], sca[0:1, 7:8], ACT.Exp), ins=[sca], outs=[sca])
    ev_s2 = inc("s", s_s)
    wait("v", s_s, ev_s2)
    op("v", lambda E: E.tensor_scalar(sca[0:1, 9:10], sca[0:1, 8:9], 1.0, None, ALU.add),
       ins=[sca], outs=[sca])
    op("v", lambda E: E.reciprocal(sca[0:1, 10:11], sca[0:1, 9:10]), ins=[sca], outs=[sca])
    op("v", lambda E: E.tensor_scalar(ccv[:, :], oh_sb[:, :], sca[0:1, 10:11], None, ALU.mult),
       ins=[oh_sb, sca], outs=[ccv])
    ev_ccv = inc("v", s_v)
    wait("g", s_v, ev_ccv)
    dma("g", cc_in[:, :].opt(), ccv[:, :], s_c, outs=[])
    b.wait("g", s_c, b.cnt[s_c])
    op("g", lambda E: E.collective_compute("AllReduce", ALU.add,
                                           replica_groups=[list(range(B))],
                                           ins=[cc_in.ap().opt()], outs=[cc_out.ap().opt()]),
       inc=s_cc, amt=1)
    b.wait("g", s_cc, b.cnt[s_cc])
    dma("g", alldwc[:, :], cc_out[:, :].opt(), s_c)
    ev_adwc = b.cnt[s_c]
    p2_load(0)
    p2_load(1)
    p2_preamble(0)

    wait("v", s_c, ev_adwc)
    op("v", lambda E: E.reduce_sum(sca[0:1, 11:12], alldwc[:, :], axis=AX.X),
       ins=[alldwc, sca], outs=[sca])
    op("v", lambda E: E.tensor_scalar(sca[0:1, 12:13], sca[0:1, 11:12], 0.125, None, ALU.mult),
       ins=[sca], outs=[sca])
    op("v", lambda E: E.tensor_scalar(sca[0:1, 13:14], sca[0:1, 12:13], 0.99, None, ALU.is_gt),
       ins=[sca], outs=[sca])
    op("v", lambda E: E.tensor_scalar(sca[0:1, 14:15], sca[0:1, 12:13], 0.01, None, ALU.is_lt),
       ins=[sca], outs=[sca])
    op("v", lambda E: E.tensor_scalar(sca[0:1, 15:16], sca[0:1, 13:14], -1.0, 1.0,
                                      ALU.mult, ALU.add), ins=[sca], outs=[sca])
    op("v", lambda E: E.tensor_scalar(sca[0:1, 16:17], sca[0:1, 14:15], -1.0, 1.0,
                                      ALU.mult, ALU.add), ins=[sca], outs=[sca])
    op("v", lambda E: E.tensor_tensor(sca[0:1, 17:18], sca[0:1, 15:16], sca[0:1, 16:17],
                                      ALU.mult), ins=[sca], outs=[sca])
    op("v", lambda E: E.tensor_tensor(sca[0:1, 18:19], sca[0:1, 17:18], sca[0:1, 10:11],
                                      ALU.mult), ins=[sca], outs=[sca])
    op("v", lambda E: E.tensor_tensor(sca[0:1, 19:20], sca[0:1, 18:19], sca[0:1, 13:14],
                                      ALU.add), ins=[sca], outs=[sca])
    op("v", lambda E: E.tensor_scalar(sca[0:1, 20:21], sca[0:1, 19:20], 0.1, None, ALU.mult),
       ins=[sca], outs=[sca])
    op("v", lambda E: E.tensor_scalar(sca[0:1, 21:22], sca[0:1, 19:20], -1.0, 1.0,
                                      ALU.mult, ALU.add), ins=[sca], outs=[sca])
    ev_coef = inc("v", s_v)
    st["out_rd"] = ev_coef
    wait("g", s_v, ev_coef)
    dma("g", scr2[:, :].opt(), sca[0:1, 20:22], s_c, outs=[])
    b.wait("g", s_c, b.cnt[s_c])
    dma("g", c01b[:, :], scr2.ap()[0:1, 0:1].partition_broadcast(128), s_c)
    dma("g", omcb[:, :], scr2.ap()[0:1, 1:2].partition_broadcast(128), s_c)
    ev_bc = b.cnt[s_c]

    def tail_post(p):
        i0, ar0 = p["i"], p["ar"]
        wait("v", s_s, p["ev_exp"])
        wait("v", s_t, p["ev_wo_t"])
        if i0 == 0:
            wait("v", s_c, TK["bao"])
        op("v", lambda E, i0=i0: E.scalar_tensor_tensor(outbuf[:, :], ps_out[:, :], 1.0,
                                                        xbuf[:, i0 % 2, :], ALU.mult, ALU.add),
           ins=[xbuf, outbuf], outs=[outbuf])
        op("v", lambda E, ar0=ar0: E.tensor_tensor(outbuf[:, :], outbuf[:, :], ar0[:, :],
                                                   ALU.add), ins=[outbuf, ar0], outs=[outbuf])
        op("v", lambda E: E.tensor_tensor(outbuf[:, :], outbuf[:, :], bao_rep[:, :],
                                          ALU.add), ins=[outbuf, bao_rep], outs=[outbuf])
        ev_fin = inc("v", s_v)
        st["out_rd"] = ev_fin
        st["xfree"][i0] = ev_fin
        wait("g", s_v, ev_fin)
        dma("g", out_hbm[i0 * 128:(i0 + 1) * 128, :], outbuf[:, :], s_o, outs=[])
        st["store"][i0] = b.cnt[s_o]
        p2_load(i0 + 2)

    halves_nxt = tr_phase(arena[0][:, :], xnh_fm[:, :], xnl_fm[:, :],
                          src_wait=[(s_v, ev_p2_xn[0])], hi_eng="v")
    prev = None
    for i in range(n_tiles):
        ar = arena[i % 2]
        halves = halves_nxt
        for c in range(NBLK):
            ev_ssm, ev_f1h0, ev_grp = grp_ssm_ffn1(i, c, halves[c // 4])
            blend(i, c, ev_ssm)
        ev_ffn1 = b.cnt[s_t]

        wait("s", s_t, ev_f1h0)
        op("s", lambda E: E.copy(u0[:, 0:512], ps_h[:, 0:512]), ins=[u0], outs=[u0])
        ev_h32a = inc("s", s_s)
        wait("s", s_t, ev_ffn1)
        op("s", lambda E: E.copy(u0[:, 512:1024], ps_h[:, 512:1024]), ins=[u0], outs=[u0])
        ev_h32 = inc("s", s_s)
        h1_halves = tr_phase(u0[:, :], h1fm_h, h1fm_l, src_wait=[(s_s, ev_h32a)],
                             src_wait2=[(s_s, ev_h32)])

        for c in range(NBLK - 1):
            wide_grp(h1h, h1l, c, fwh[1], fwl[1], ps_h, c == 0, False,
                     hi_wait=h1_halves[c // 4][0], lo_wait=h1_halves[c // 4][1])

        if prev is not None:
            tail_post(prev)

        ln_stats(ar)
        op("v", lambda E, ar=ar: E.tensor_scalar(ar[:, :], ar[:, :], negmu[:, 0:1],
                                                 rstd[:, 0:1], ALU.add, ALU.mult),
           ins=[ar, negmu, rstd], outs=[ar])
        ev_norm = inc("v", s_v)

        n_halves = tr_phase(ar[:, :], nfm_h, nfm_l, src_wait=[(s_v, ev_norm)])
        ev_ffn2 = wide_grp(h1h, h1l, NBLK - 1, fwh[1], fwl[1], ps_h, False, True,
                           hi_wait=h1_halves[1][0], lo_wait=h1_halves[1][1],
                           inc_last=s_t, h0_first=True)
        wait("s", s_t, ev_ffn2[0])
        op("s", lambda E: E.copy(u0[:, 0:512], ps_h[:, 0:512]), ins=[u0], outs=[u0])
        ev_h32c = inc("s", s_s)
        wait("s", s_t, ev_ffn2[1])
        op("s", lambda E: E.copy(u0[:, 512:1024], ps_h[:, 512:1024]), ins=[u0], outs=[u0])
        ev_h32b = inc("s", s_s)

        p2_preamble(i + 1)

        for c in range(NBLK - 1):
            ew = list(n_halves[c // 4][0])
            if c == 0:
                ew.append((s_v, st["out_rd"]))
            wide_grp(nfh, nfl, c, woh, wol, ps_out, c == 0, False,
                     hi_wait=ew, lo_wait=n_halves[c // 4][1])

        h2_halves = tr_phase(u0[:, :], h2fm_h, h2fm_l, src_wait=[(s_s, ev_h32c)],
                             src_wait2=[(s_s, ev_h32b)])
        ev_wo_t = wide_grp(nfh, nfl, NBLK - 1, woh, wol, ps_out, False, True,
                           hi_wait=n_halves[1][0], lo_wait=n_halves[1][1],
                           inc_last=s_t)[1]

        for c in range(2):
            wide_grp(h2h, h2l, c, fwh[2], fwl[2], ps_h, c == 0, False,
                     hi_wait=h2_halves[c // 4][0], lo_wait=h2_halves[c // 4][1])
        if i + 1 < n_tiles:
            halves_nxt = tr_phase(arena[(i + 1) % 2][:, :], xnh_fm[:, :], xnl_fm[:, :],
                                  src_wait=[(s_v, ev_p2_xn[i + 1])], hi_eng="v")
        for c in range(2, NBLK):
            ev_ffn3 = wide_grp(h2h, h2l, c, fwh[2], fwl[2], ps_h, False, c == NBLK - 1,
                               hi_wait=h2_halves[c // 4][0], lo_wait=h2_halves[c // 4][1],
                               inc_last=s_t if c == NBLK - 1 else None,
                               h0_first=(c == NBLK - 1))

        op("s", lambda E: E.activation(sca[0:1, 30:31], sca[0:1, 30:31], ACT.Tanh),
           ins=[sca], outs=[sca])
        wait("v", s_t, ev_ffn3[0])
        if i == 0:
            wait("v", s_w, ev_w0)
        wait("v", s_o, st["store"][i - 1] if i >= 1 else 0)
        op("v", lambda E: E.tensor_tensor(outbuf[:, H0], ps_h[:, H0], fb_rep[:, H0],
                                          ALU.add), ins=[fb_rep, outbuf], outs=[outbuf])
        ev_fb0 = inc("v", s_v)
        st["psh_rd0"] = ev_fb0
        wait("v", s_t, ev_ffn3[1])
        op("v", lambda E: E.tensor_tensor(outbuf[:, H1], ps_h[:, H1], fb_rep[:, H1],
                                          ALU.add), ins=[fb_rep, outbuf], outs=[outbuf])
        ev_fb = inc("v", s_v)
        st["psh_rd"] = ev_fb
        wait("s", s_v, ev_fb)
        op("s", lambda E, ar=ar: E.activation(ar[:, :], outbuf[:, :], ACT.Tanh),
           ins=[outbuf], outs=[ar])
        ev_exp = inc("s", s_s)
        op("s", lambda E: E.activation(sca[0:1, 31:32], epst[0:1, 0:1], ACT.Ln,
                                       bias=epst[0:1, 0:1]), ins=[sca, epst], outs=[sca])
        prev = {"i": i, "ar": ar, "ev_wo_t": ev_wo_t, "ev_exp": ev_exp}

    tail_post(prev)
    b.wait("g", s_o, b.cnt[s_o])

    block = es.enter_context(nc.Block())
    with es:
        @block.sync
        def _(E):
            for f in b.SCH["y"]:
                f(E)

        @block.vector
        def _(E):
            for f in b.SCH["v"]:
                f(E)

        @block.scalar
        def _(E):
            for f in b.SCH["s"]:
                f(E)

        @block.tensor
        def _(E):
            for f in b.SCH["t"]:
                f(E)

        @block.gpsimd
        def _(E):
            for f in b.SCH["g"]:
                f(E)
    return nc


_CACHE = {}


def _get_nc(n_tiles):
    if n_tiles not in _CACHE:
        _CACHE[n_tiles] = build_nc(n_tiles)
    return _CACHE[n_tiles]


def kernel(L0_attn_x, L0_attn_W_q, L0_attn_W_k, L0_attn_W_v, L0_attn_W_o,
           L0_attn_SSM_A, L0_attn_SSM_B, L0_attn_SSM_C, L0_attn_ln_gamma,
           L0_attn_ln_beta, L0_ffn_weight, L0_ffn_aux_weight1, L0_ffn_aux_weight2,
           L0_ffn_bias):
    x = np.ascontiguousarray(np.asarray(L0_attn_x, dtype=np.float32))
    Bsz, S_, D_ = x.shape
    assert D_ == D and Bsz == B
    n_tiles = S_ // 128
    idx = np.arange(NBLK)

    def diag(M):
        M = np.asarray(M, dtype=np.float32)
        return np.ascontiguousarray(M.reshape(NBLK, 128, NBLK, 128)[idx, :, idx, :])

    common = {
        "wq": np.ascontiguousarray(np.asarray(L0_attn_W_q, dtype=np.float32)),
        "wk": np.ascontiguousarray(np.asarray(L0_attn_W_k, dtype=np.float32)),
        "wv": np.ascontiguousarray(np.asarray(L0_attn_W_v, dtype=np.float32)),
        "ad": diag(L0_attn_SSM_A),
        "bd": diag(L0_attn_SSM_B),
        "cd": diag(L0_attn_SSM_C),
        "wo": np.ascontiguousarray(np.asarray(L0_attn_W_o, dtype=np.float32)),
        "gamma": np.ascontiguousarray(np.asarray(L0_attn_ln_gamma, dtype=np.float32)),
        "beta": np.ascontiguousarray(np.asarray(L0_attn_ln_beta, dtype=np.float32)),
        "f1": np.ascontiguousarray(np.asarray(L0_ffn_weight, dtype=np.float32)),
        "f2": np.ascontiguousarray(np.asarray(L0_ffn_aux_weight1, dtype=np.float32)),
        "f3": np.ascontiguousarray(np.asarray(L0_ffn_aux_weight2, dtype=np.float32)),
        "fb": np.ascontiguousarray(np.asarray(L0_ffn_bias, dtype=np.float32)),
        "eye": np.eye(128, dtype=np.float32),
    }
    eyeB = np.eye(B, dtype=np.float32)
    in_maps = [dict(common, x=np.ascontiguousarray(x[bb]),
                    onehot=np.ascontiguousarray(eyeB[bb:bb + 1]))
               for bb in range(B)]
    nc = _get_nc(n_tiles)
    res = run_bass_kernel_spmd(nc, in_maps, core_ids=list(range(B)))
    return np.stack([np.asarray(r["out"]) for r in res.results], axis=0)


# version 36
# speedup vs baseline: 1.0058x; 1.0058x over previous
import contextlib
import numpy as np

import concourse.bass as bass
import concourse.mybir as mybir
from concourse.bass_utils import run_bass_kernel_spmd

F32 = mybir.dt.float32
BF16 = mybir.dt.bfloat16
AX = mybir.AxisListType
ALU = mybir.AluOpType
ACT = mybir.ActivationFunctionType

B = 8
D = 1024
NBLK = 8
EPS = 1e-5
CLIP = 88.0
H0 = slice(0, 512)
H1 = slice(512, 1024)


def _nm(ap):
    n = ap.name
    return n() if callable(n) else n


class Bld:

    def __init__(self):
        self.SCH = {k: [] for k in ("t", "v", "s", "y", "g")}
        self.cnt = {}
        self.dirty = {k: set() for k in self.SCH}
        self.wm = {}

    def op(self, k, f, ins=(), outs=(), inc=None, amt=1):
        nm = {_nm(a) for a in ins} | {_nm(a) for a in outs}
        if k != "t" and (nm & self.dirty[k]):
            self.SCH[k].append(lambda E: E.drain())
            self.dirty[k].clear()
        if inc is not None:
            self.cnt[inc] = self.cnt.get(inc, 0) + amt
            self.SCH[k].append(lambda E, f=f, inc=inc, amt=amt: f(E).then_inc(inc, amt))
            ret = self.cnt[inc]
        else:
            self.SCH[k].append(lambda E, f=f: f(E))
            ret = None
        self.dirty[k] |= {_nm(a) for a in outs}
        return ret

    def wait(self, k, sem, val):
        if not val:
            return
        assert val <= self.cnt.get(sem, 0), f"wait beyond scheduled ({val} > {self.cnt.get(sem, 0)})"
        wm = self.wm.setdefault(k, {})
        if wm.get(sem, -1) >= val:
            return
        wm[sem] = val
        self.SCH[k].append(lambda E, sem=sem, val=val: E.wait_ge(sem, val))

    def inc(self, k, sem, amt=1):
        self.cnt[sem] = self.cnt.get(sem, 0) + amt
        self.SCH[k].append(lambda E, sem=sem, amt=amt: E.drain().then_inc(sem, amt))
        self.dirty[k].clear()
        return self.cnt[sem]


def build_nc(n_tiles):
    nc = bass.Bass("TRN2", target_bir_lowering=False, debug=False)
    S_loc = n_tiles * 128
    assert n_tiles % 2 == 0 and n_tiles >= 4

    x_hbm = nc.dram_tensor("x", [S_loc, D], F32, kind="ExternalInput").ap()
    w_hbm = {}
    for name, shp in (("wq", [NBLK, 128, 128]), ("wk", [NBLK, 128, 128]),
                      ("wv", [NBLK, 128, 128]), ("ad", [NBLK, 128, 128]),
                      ("bd", [NBLK, 128, 128]), ("cd", [NBLK, 128, 128]),
                      ("wo", [D, D]), ("gamma", [D]), ("beta", [D]),
                      ("f1", [D, D]), ("f2", [D, D]), ("f3", [D, D]), ("fb", [D]),
                      ("eye", [128, 128]), ("onehot", [1, B])):
        w_hbm[name] = nc.dram_tensor(name, shp, F32, kind="ExternalInput").ap()
    out_hbm = nc.dram_tensor("out", [S_loc, D], F32, kind="ExternalOutput").ap()
    cc_in = nc.dram_tensor("cc_in", [1, B], F32)
    cc_out = nc.dram_tensor("cc_out", [1, B], F32)
    scr2 = nc.dram_tensor("scr2", [1, 2], F32)
    scrb = nc.dram_tensor("scrb", [1, D], F32)

    es = contextlib.ExitStack()
    sb = lambda n, s, d=F32: es.enter_context(nc.sbuf_tensor(n, s, d))

    gh = sb("gh", [128, NBLK, 387], BF16)
    gl = sb("gl", [128, NBLK, 387], BF16)
    woh = sb("woh", [128, NBLK, D], BF16)
    wol = sb("wol", [128, NBLK, D], BF16)
    fwh = [sb(f"f{j}h", [128, NBLK, D], BF16) for j in (1, 2, 3)]
    fwl = [sb(f"f{j}l", [128, NBLK, D], BF16) for j in (1, 2, 3)]
    bao_rep = sb("bao_rep", [128, D], F32)
    fb_rep = sb("fb_rep", [128, D], F32)
    bao_row = sb("bao_row", [1, D], F32)
    ones_row = sb("ones_row", [1, 128], BF16)
    eye32 = sb("eye32", [128, 128], F32)
    gm = sb("gm", [128, NBLK], F32)
    bm32 = sb("bm32", [128, NBLK], F32)
    bmh = sb("bmh", [128, NBLK], BF16)
    bml = sb("bml", [128, NBLK], BF16)

    xbuf = sb("xbuf", [128, 2, D], F32)
    xn = sb("xn", [128, D], F32)
    wstage = sb("wstage", [128, D], F32)
    fstage = sb("fstage", [128, D], F32)
    xnh_fm = sb("xnh_fm", [128, D], BF16)
    xnl_fm = sb("xnl_fm", [128, D], BF16)
    u0 = sb("u0", [128, D], F32)
    u4k_0 = sb("u4k_0", [128, 2048], BF16)
    u4k_1 = sb("u4k_1", [128, 2048], BF16)
    u4k_2 = sb("u4k_2", [128, 2048], BF16)
    outbuf = sb("outbuf", [128, D], F32)
    stats6 = sb("stats6", [128, 2, 2, 6], F32)
    mv = sb("mv", [128, 2, 2], F32)
    negmu = sb("negmu", [128, 2], F32)
    la = sb("la", [128, 2], F32)
    rstd = sb("rstd", [128, 2], F32)
    nmr = sb("nmr", [128, 2], F32)
    epst = sb("epst", [128, 1], F32)
    nhc = sb("nhc", [128, 1], F32)
    g3 = sb("g3", [128, 3], F32)
    c01b = sb("c01b", [128, 1], F32)
    omcb = sb("omcb", [128, 1], F32)
    sca = sb("sca", [1, 32], F32)
    oh_sb = sb("oh_sb", [1, B], F32)
    ccv = sb("ccv", [1, B], F32)
    alldwc = sb("alldwc", [1, B], F32)

    stg32b = sb("stg32b", [128, 771], F32)
    stg32b2 = sb("stg32b2", [128, D], F32)
    stg32_p = [wstage[:, 0:771], stg32b[:, :]]
    stgh_p = [u4k_1[:, 0:771], u4k_2[:, 0:771]]
    stgl_p = [u4k_1[:, 1024:1795], u4k_2[:, 1024:1795]]
    wraw_h_p = [u4k_2[:, 0:1024], u4k_1[:, 0:1024]]
    wraw_l_p = [u4k_2[:, 1024:2048], u4k_1[:, 1024:2048]]
    p1set = None
    nfm_h = u4k_1[:, 0:1024]
    nfm_l = u4k_1[:, 1024:2048]
    h1fm_h = u4k_2[:, 0:1024]
    h1fm_l = u4k_2[:, 1024:2048]
    h2fm_h = u4k_0[:, 0:1024]
    h2fm_l = u4k_0[:, 1024:2048]
    m_view = u0[:, :].rearrange("p (n d) -> p n d", n=NBLK)
    p1set = [(u4k_0[:, 0:1024], u4k_0[:, 1024:2048]),
             (xnh_fm[:, :], xnl_fm[:, :])]

    ps_trF = es.enter_context(nc.psum_tensor("ps_trF", [128, 1024], F32))
    ps_ssm = [es.enter_context(nc.psum_tensor(f"ps_ssm{i}", [128, 512], F32)) for i in range(2)]
    ps_out = es.enter_context(nc.psum_tensor("ps_out", [128, 1024], F32))
    ps_h = es.enter_context(nc.psum_tensor("ps_h", [128, 1024], F32))

    s_x = es.enter_context(nc.semaphore("s_x"))
    s_w = es.enter_context(nc.semaphore("s_w"))
    s_o = es.enter_context(nc.semaphore("s_o"))
    s_c = es.enter_context(nc.semaphore("s_c"))
    s_cc = es.enter_context(nc.semaphore("s_cc"))
    s_t = es.enter_context(nc.semaphore("s_t"))
    s_v = es.enter_context(nc.semaphore("s_v"))
    s_s = es.enter_context(nc.semaphore("s_s"))
    s_g = es.enter_context(nc.semaphore("s_g"))

    b = Bld()
    op, wait, inc = b.op, b.wait, b.inc

    def dma(k, dst, src, sem, outs=None, slow=False, serialize=True):
        if serialize:
            wait(k, sem, b.cnt.get(sem, 0))
        return op(k, lambda E, dst=dst, src=src, slow=slow:
                  E.dma_start(out=dst, in_=src, allow_slow_non_contiguous=slow),
                  outs=outs if outs is not None else [dst], inc=sem, amt=16)

    def mm(pdst, lhsT, rhs, start, stop, inc=None):
        return op("t", lambda E, p=pdst, a=lhsT, bb=rhs, s1=start, s2=stop:
                  E.matmul(p, a, bb, start=s1, stop=s2), inc=inc)

    def mm3(pdst, ah, al, bh, bl, start, stop, inc=None):
        mm(pdst, ah, bh, start, False)
        mm(pdst, ah, bl, False, False)
        return mm(pdst, al, bh, False, stop, inc=inc)

    def split_s(hi_dst, src, ins=()):
        op("s", lambda E, hi_dst=hi_dst, src=src: E.copy(hi_dst, src),
           ins=list(ins) + [src], outs=[hi_dst])

    def split_v(lo_dst, src, hi, ins=()):
        op("v", lambda E, lo_dst=lo_dst, src=src, hi=hi:
           E.scalar_tensor_tensor(lo_dst, src, 1.0, hi, ALU.mult, ALU.subtract),
           ins=list(ins) + [src, hi], outs=[lo_dst])

    def ln_stats(src_ap, p=0):
        for c2 in range(2):
            op("v", lambda E, c2=c2, src_ap=src_ap, p=p:
               E.bn_stats(stats6[:, p, c2, :], src_ap[:, c2 * 512:(c2 + 1) * 512]),
               ins=[src_ap], outs=[stats6])
        op("v", lambda E, p=p: E.bn_aggr(mv[:, p, :],
                                         stats6[:, p, :, :].rearrange("p c s -> p (c s)")),
           ins=[stats6], outs=[mv])
        op("v", lambda E, p=p: E.tensor_scalar(negmu[:, p:p + 1], mv[:, p, 0:1], -1.0,
                                               None, ALU.mult), ins=[mv], outs=[negmu])
        ev = inc("v", s_v)
        wait("s", s_v, ev)
        op("s", lambda E, p=p: E.activation(la[:, p:p + 1], mv[:, p, 1:2], ACT.Ln,
                                            bias=epst[:, 0:1]), ins=[mv, epst], outs=[la])
        op("s", lambda E, p=p: E.activation(rstd[:, p:p + 1], la[:, p:p + 1], ACT.Exp,
                                            scale=nhc[:, 0:1]), ins=[la, nhc], outs=[rstd])
        ev_s = inc("s", s_s)
        wait("v", s_s, ev_s)

    slot_free = {}

    def slot_wait(k, p):
        for sem, val in slot_free.get(_nm(p[:, 0:1]), []):
            wait(k, sem, val)

    def tr_phase(src32_ap, hi_dst, lo_dst, src_wait=(), src_wait2=(), hi_eng="s"):
        slot_wait("t", ps_trF)
        for sem, val in src_wait:
            wait("t", sem, val)
        halves = []
        for half in range(2):
            if half == 1:
                for sem, val in src_wait2:
                    wait("t", sem, val)
            ev_t = None
            for c in range(4 * half, 4 * half + 4):
                ev_t = op("t", lambda E, c=c, src32_ap=src32_ap:
                          E.transpose(ps_trF[:, c * 128:(c + 1) * 128],
                                      src32_ap[:, c * 128:(c + 1) * 128], eye32[:, :]),
                          inc=s_t if c % 4 == 3 else None)
            hs = slice(half * 512, (half + 1) * 512)
            if hi_eng == "s":
                wait("s", s_t, ev_t)
                split_s(hi_dst[:, hs], ps_trF[:, hs])
                se = inc("s", s_s)
                hi_ev = (s_s, se)
            else:
                wait("v", s_t, ev_t)
                op("v", lambda E, hs=hs, hi_dst=hi_dst:
                   E.tensor_copy(hi_dst[:, hs], ps_trF[:, hs]), outs=[hi_dst])
                hi_ev = (s_v, inc("v", s_v))
            wait("v", s_t, ev_t)
            wait("v", hi_ev[0], hi_ev[1])
            split_v(lo_dst[:, hs], ps_trF[:, hs], hi_dst[:, hs])
            ve = inc("v", s_v)
            halves.append(([hi_ev], [(s_v, ve)]))
        slot_free["trF:h0"] = halves[0][0] + halves[0][1]
        slot_free["trF:h1"] = halves[1][0] + halves[1][1]
        slot_free[_nm(ps_trF[:, 0:1])] = (halves[0][0] + halves[0][1]
                                          + halves[1][0] + halves[1][1])
        return halves

    dma("y", eye32[:, :], w_hbm["eye"][:, :], s_w)
    dma("y", gm[:, :], w_hbm["gamma"].rearrange("(c d) -> d c", d=128), s_w, slow=True)
    dma("y", bm32[:, :], w_hbm["beta"].rearrange("(c d) -> d c", d=128), s_w, slow=True)
    dma("y", fb_rep[:, :], w_hbm["fb"].unsqueeze(0).partition_broadcast(128), s_w)
    dma("y", oh_sb[:, :], w_hbm["onehot"][:, :], s_w)
    ev_w0 = b.cnt[s_w]

    for f in (lambda E: E.memset(epst[:, :], EPS),
              lambda E: E.memset(nhc[:, :], -0.5),
              lambda E: E.memset(ones_row[:, :], 1.0),
              lambda E: E.memset(sca[:, :], 0.0)):
        op("v", f, outs=[epst, nhc, ones_row, sca])
    for stg in stg32_p:
        for col in (256, 513, 770):
            op("v", lambda E, stg=stg, col=col: E.memset(stg[:, col:col + 1], 1.0),
               ins=[stg], outs=[stg])
    ev_ones0 = inc("v", s_v)
    op("v", lambda E: E.memset(sca[0:1, 0:1], float(S_loc) + 1.0), ins=[sca], outs=[sca])

    ev_s0 = inc("s", s_s)
    wait("v", s_w, ev_w0)
    ev_v0 = inc("v", s_v)

    arena = [xn, wstage]
    p1arena = [xn, outbuf]

    ev_p1_split = [None] * n_tiles
    ev_p1_xrd = [None] * n_tiles
    ev_p1_gram = [None] * n_tiles

    def p1_load(i):
        if i >= 2:
            wait("y", s_s, ev_p1_xrd[i - 2])
        dma("y", xbuf[:, i % 2, :], x_hbm[i * 128:(i + 1) * 128, :], s_x, outs=[xbuf])

    def p1_preamble(i):
        p = i % 2
        wait("v", s_x, 16 * (i + 1))
        for c2 in range(2):
            op("v", lambda E, c2=c2, i=i, p=p:
               E.bn_stats(stats6[:, p, c2, :], xbuf[:, i % 2, c2 * 512:(c2 + 1) * 512]),
               ins=[xbuf], outs=[stats6])
        op("v", lambda E, p=p: E.bn_aggr(mv[:, p, :],
                                         stats6[:, p, :, :].rearrange("p c s -> p (c s)")),
           ins=[stats6], outs=[mv])
        op("v", lambda E, p=p: E.tensor_scalar(negmu[:, p:p + 1], mv[:, p, 0:1], -1.0,
                                               None, ALU.mult), ins=[mv], outs=[negmu])
        ev_v = inc("v", s_v)
        wait("s", s_v, ev_v)
        op("s", lambda E, p=p: E.activation(la[:, p:p + 1], mv[:, p, 1:2], ACT.Ln,
                                            bias=epst[:, 0:1]), ins=[mv, epst], outs=[la])
        op("s", lambda E, p=p: E.activation(rstd[:, p:p + 1], la[:, p:p + 1], ACT.Exp,
                                            scale=nhc[:, 0:1]), ins=[la, nhc], outs=[rstd])
        op("s", lambda E, p=p: E.activation(nmr[:, p:p + 1], negmu[:, p:p + 1],
                                            ACT.Identity, scale=rstd[:, p:p + 1]),
           ins=[negmu, rstd], outs=[nmr])
        ar = p1arena[i % 2]
        if i >= 2:
            wait("s", s_t, ev_p1_gram[i - 2])
        op("s", lambda E, i=i, ar=ar, p=p: E.activation(ar[:, :], xbuf[:, i % 2, :],
                                                        ACT.Identity,
                                                        bias=nmr[:, p:p + 1],
                                                        scale=rstd[:, p:p + 1]),
           ins=[xbuf, nmr, rstd], outs=[ar])
        ev_xn = inc("s", s_s)
        ev_p1_xrd[i] = ev_xn
        ev_p1_split[i] = ev_xn

    def p1_gram(i):
        ar = p1arena[i % 2]
        wait("t", s_s, ev_p1_split[i])
        for n in range(NBLK):
            csl = slice(n * 128, (n + 1) * 128)
            first = (i == 0) and (n in (0, 4))
            last = (i == n_tiles - 1) and (n in (3, 7))
            ev_p1_gram[i] = mm(ps_out[:, csl], ar[:, csl], ar[:, csl], first, last,
                               inc=s_t if n == NBLK - 1 else None)

    TK = {"split": (ev_s0, ev_v0), "evac_s": None, "evac_v": None, "gmm": None,
          "bias_t": None, "wo": None, "f": None, "bao": None, "gmm_final": None,
          "fp": {0: None, 1: None}}

    def g_task(n):
        stg32 = stg32_p[n % 2]
        wait("y", s_v, ev_ones0)
        if TK["gmm"] is not None:
            wait("y", s_t, TK["gmm"])
        wait("y", s_w, b.cnt.get(s_w, 0))
        for dst0, srcn in ((0, "wq"), (128, "ad"), (257, "wk"), (385, "bd"),
                           (514, "wv"), (642, "cd")):
            dma("y", stg32[:, dst0:dst0 + 128], w_hbm[srcn][n], s_w, outs=[stg32],
                serialize=False)
        ev_ld = b.cnt[s_w]
        return lambda: g_back(n, ev_ld)

    def g_back(n, ev_ld):
        stg32 = stg32_p[n % 2]
        wait("t", s_w, ev_ld)
        wait("t", s_v, ev_ones0)
        if TK["evac_s"] is not None:
            wait("t", s_s, TK["evac_s"])
            wait("t", s_v, TK["evac_v"])
        mm(ps_ssm[0][:, 0:257], stg32[:, 0:128], stg32[:, 128:385], True, True)
        mm(ps_ssm[1][:, 0:129], stg32[:, 257:385], stg32[:, 385:514], True, True)
        ev_gmm = TK["gmm"] = mm(ps_ssm[1][:, 129:258], stg32[:, 514:642],
                                stg32[:, 642:771], True, True, inc=s_t)
        evmap = ((0, ps_ssm[0][:, 0:128]), (384, ps_ssm[0][:, 128:129]),
                 (128, ps_ssm[1][:, 0:128]), (385, ps_ssm[1][:, 128:129]),
                 (256, ps_ssm[1][:, 129:257]), (386, ps_ssm[1][:, 257:258]))
        wait("s", s_t, ev_gmm)
        for dc, pslc in evmap:
            w_ = pslc.shape[1]
            op("s", lambda E, n=n, dc=dc, p=pslc, w_=w_: E.copy(gh[:, n, dc:dc + w_], p),
               outs=[gh])
        TK["evac_s"] = inc("s", s_s)
        wait("v", s_t, ev_gmm)
        wait("v", s_s, TK["evac_s"])
        for dc, pslc in evmap:
            w_ = pslc.shape[1]
            op("v", lambda E, n=n, dc=dc, p=pslc, w_=w_:
               E.scalar_tensor_tensor(gl[:, n, dc:dc + w_], p, 1.0, gh[:, n, dc:dc + w_],
                                      ALU.mult, ALU.subtract), ins=[gh], outs=[gl])
        op("v", lambda E, n=n: E.tensor_copy(m_view[:, n, :], ps_ssm[0][:, 129:257]),
           outs=[u0])
        TK["evac_v"] = inc("v", s_v)
        return None

    def wo_task(c):
        if TK["wo"] is None:
            TK["wo"] = (None, TK["split"][1])
            TK["gmm_final"] = TK["gmm"]
        wait("y", s_v, TK["wo"][1])
        wait("y", s_t, TK["gmm_final"])
        dma("y", wstage[:, :], w_hbm["wo"][c * 128:(c + 1) * 128, :], s_w)
        if TK["bias_t"] is not None:
            wait("y", s_t, TK["bias_t"])
        else:
            wait("y", s_t, TK["gmm_final"])
        dma("y", stg32b2[:, :], w_hbm["wo"][c * 128:(c + 1) * 128, :], s_w)
        ev_ld = b.cnt[s_w]
        wait("v", s_w, ev_ld)
        op("v", lambda E, c=c: E.tensor_scalar(wstage[:, :], wstage[:, :], gm[:, c:c + 1],
                                               None, ALU.mult), ins=[wstage, gm], outs=[wstage])
        ev_sc = inc("v", s_v)
        return lambda: wo_back(c, ev_ld, ev_sc)

    def wo_back(c, ev_ld, ev_sc):
        wait("t", s_w, ev_ld)
        mm(ps_h[0:1, H0], bm32[:, c:c + 1], stg32b2[:, H0],
           start=(c == 0), stop=(c == NBLK - 1))
        TK["bias_t"] = mm(ps_h[0:1, H1], bm32[:, c:c + 1], stg32b2[:, H1],
                          start=(c == 0), stop=(c == NBLK - 1), inc=s_t)
        wait("s", s_v, ev_sc)
        op("s", lambda E, c=c: E.copy(woh[:, c, :], wstage[:, :]), ins=[wstage], outs=[woh])
        ev_wh = inc("s", s_s)
        wait("v", s_s, ev_wh)
        op("v", lambda E, c=c: E.scalar_tensor_tensor(wol[:, c, :], wstage[:, :], 1.0,
                                                      woh[:, c, :], ALU.mult, ALU.subtract),
           ins=[wstage, woh], outs=[wol])
        TK["wo"] = (ev_wh, inc("v", s_v))
        return None

    def bias_evac_task():
        TK["f"] = TK["wo"]

        def back():
            wait("v", s_t, TK["bias_t"])
            op("v", lambda E: E.tensor_copy(bao_row[:, :], ps_h[0:1, :]), outs=[bao_row])
            ev_bs = inc("v", s_v)
            wait("g", s_v, ev_bs)
            dma("g", scrb[:, :].opt(), bao_row[:, :], s_c, outs=[])
            dma("g", bao_rep[:, :], scrb.ap()[0:1, :].partition_broadcast(128), s_c)
            TK["bao"] = b.cnt[s_c]
        return back

    fst_p = [wstage, fstage]

    def ffn_task(j, c):
        k = j * NBLK + c
        wst = fst_p[k % 2]
        prev = TK["f"] if k == 0 else TK["fp"][k % 2]
        if prev is not None:
            wait("y", s_s, prev[0])
            wait("y", s_v, prev[1])
        dma("y", wst[:, :], w_hbm[f"f{j + 1}"][c * 128:(c + 1) * 128, :], s_w)
        ev_ld = b.cnt[s_w]
        wait("s", s_w, ev_ld)
        split_s(fwh[j][:, c, :], wst[:, :])
        ev_fh = inc("s", s_s)

        def back():
            wait("v", s_w, ev_ld)
            wait("v", s_s, ev_fh)
            split_v(fwl[j][:, c, :], wst[:, :], fwh[j][:, c, :])
            TK["fp"][k % 2] = (ev_fh, inc("v", s_v))
        return back

    TASKS = ([(lambda n=n: g_task(n)) for n in range(NBLK)]
             + [(lambda c=c: wo_task(c)) for c in range(NBLK)]
             + [bias_evac_task]
             + [(lambda j=j, c=c: ffn_task(j, c)) for j in range(3) for c in range(NBLK)])
    MAXLEAD = [1, 1] + [2] * (NBLK - 2) + [1] * NBLK + [1] + [2] * (3 * NBLK)

    p1_load(0)
    p1_load(1)
    p1_preamble(0)
    p1_preamble(1)
    _ntask = 0
    _nback = 0
    _backs = []
    for i in range(n_tiles):
        p1_gram(i)
        if i + 2 < n_tiles:
            p1_load(i + 2)
            p1_preamble(i + 2)
        tgt_b = i * len(TASKS) // n_tiles
        tgt_f = (i + 1) * len(TASKS) // n_tiles
        while _nback < min(_ntask, tgt_b):
            f = _backs[_nback]
            if f is not None:
                f()
            _nback += 1
        while _ntask < tgt_f:
            while _nback < _ntask + 1 - MAXLEAD[_ntask]:
                f = _backs[_nback]
                if f is not None:
                    f()
                _nback += 1
            _backs.append(TASKS[_ntask]())
            _ntask += 1
    while _nback < len(TASKS):
        f = _backs[_nback]
        if f is not None:
            f()
        _nback += 1



    xfh = xnh_fm[:, :].rearrange("p (n d) -> p n d", n=NBLK)
    xfl = xnl_fm[:, :].rearrange("p (n d) -> p n d", n=NBLK)
    nfh = nfm_h.rearrange("p (n d) -> p n d", n=NBLK)
    nfl = nfm_l.rearrange("p (n d) -> p n d", n=NBLK)
    h1h = h1fm_h.rearrange("p (n d) -> p n d", n=NBLK)
    h1l = h1fm_l.rearrange("p (n d) -> p n d", n=NBLK)
    h2h = h2fm_h.rearrange("p (n d) -> p n d", n=NBLK)
    h2l = h2fm_l.rearrange("p (n d) -> p n d", n=NBLK)

    st = {"psh_rd": None,
          "psh_rd0": None,
          "out_rd": None,
          "store": [0] * (n_tiles + 1),
          "xfree": [None] * (n_tiles + 2)}
    ev_p2_xn = [None] * (n_tiles + 1)

    def p2_load(i):
        if i >= n_tiles:
            return
        if i >= 2:
            wait("y", s_v, st["xfree"][i - 2])
        else:
            wait("y", s_s, ev_p1_xrd[n_tiles - 2 + i])
        dma("y", xbuf[:, i % 2, :], x_hbm[i * 128:(i + 1) * 128, :], s_x, outs=[xbuf])

    def p2_preamble(i):
        if i >= n_tiles:
            return
        wait("v", s_x, 16 * (n_tiles + i + 1))
        ln_stats(xbuf[:, i % 2, :])
        ar = arena[i % 2]
        op("v", lambda E, i=i, ar=ar: E.tensor_scalar(ar[:, :], xbuf[:, i % 2, :],
                                                      negmu[:, 0:1], rstd[:, 0:1],
                                                      ALU.add, ALU.mult),
           ins=[xbuf, negmu, rstd], outs=[ar])
        ev_p2_xn[i] = inc("v", s_v)

    ssm_slots = [(ps_ssm[0], 0, None), (ps_ssm[1], 0, None),
                 (ps_trF, 0, "trF:h0"), (ps_trF, 512, "trF:h1")]
    ssm_slots_t0 = [(ps_ssm[0], 0, None), (ps_ssm[1], 0, None),
                    (ps_out, 0, "out:h0"), (ps_out, 512, "out:h1")]

    def grp_ssm_ffn1(i, c, spl):
        bank, off, skey = (ssm_slots_t0 if i == 0 else ssm_slots)[c % 4]
        for sem, val in spl[0]:
            wait("t", sem, val)
        if skey:
            for sem, val in slot_free.get(skey, []):
                wait("t", sem, val)
        else:
            slot_wait("t", bank)
        Xh, Xl = xfh[:, c, :], xfl[:, c, :]
        bk = bank[:, off:off + 387]
        mm(bk, Xh, gh[:, c, :], True, False)
        mm(bk, Xh, gl[:, c, :], False, False)
        for sem, val in spl[1]:
            wait("t", sem, val)
        ev_ssm = mm(bk, Xl, gh[:, c, :], False, True, inc=s_t)
        if c == 0:
            if st["psh_rd0"]:
                wait("t", s_v, st["psh_rd0"])
            mm(ps_h[:, H0], Xh, fwh[0][:, c, H0], True, False)
            mm(ps_h[:, H0], Xh, fwl[0][:, c, H0], False, False)
            mm(ps_h[:, H0], Xl, fwh[0][:, c, H0], False, False)
            if st["psh_rd"]:
                wait("t", s_v, st["psh_rd"])
            mm(ps_h[:, H1], Xh, fwh[0][:, c, H1], True, False)
            mm(ps_h[:, H1], Xh, fwl[0][:, c, H1], False, False)
            ev_grp = mm(ps_h[:, H1], Xl, fwh[0][:, c, H1], False, False, inc=s_t)
            return ev_ssm, None, ev_grp
        if c == NBLK - 1:
            mm(ps_h[:, H0], Xh, fwh[0][:, c, H0], False, False)
            mm(ps_h[:, H0], Xh, fwl[0][:, c, H0], False, False)
            ev_h0 = mm(ps_h[:, H0], Xl, fwh[0][:, c, H0], False, True, inc=s_t)
            mm(ps_h[:, H1], Xh, fwh[0][:, c, H1], False, False)
            mm(ps_h[:, H1], Xh, fwl[0][:, c, H1], False, False)
            ev_grp = mm(ps_h[:, H1], Xl, fwh[0][:, c, H1], False, True, inc=s_t)
            return ev_ssm, ev_h0, ev_grp
        mm(ps_h[:, H0], Xh, fwh[0][:, c, H0], c == 0, False)
        mm(ps_h[:, H1], Xh, fwh[0][:, c, H1], c == 0, False)
        mm(ps_h[:, H0], Xh, fwl[0][:, c, H0], False, False)
        mm(ps_h[:, H1], Xh, fwl[0][:, c, H1], False, False)
        mm(ps_h[:, H0], Xl, fwh[0][:, c, H0], False, False)
        ev_grp = mm(ps_h[:, H1], Xl, fwh[0][:, c, H1], False, False, inc=s_t)
        return ev_ssm, None, ev_grp

    def blend(i, c, ev_ssm):
        bank, off, skey = (ssm_slots_t0 if i == 0 else ssm_slots)[c % 4]
        ar = arena[i % 2]
        wait("v", s_t, ev_ssm)
        if i == 0 and c == 0:
            wait("v", s_c, ev_bc)
        op("v", lambda E, bank=bank, off=off: E.tensor_scalar(
            g3[:, :], bank[:, off + 384:off + 387], c01b[:, 0:1], omcb[:, 0:1],
            ALU.mult, ALU.add), ins=[c01b, omcb], outs=[g3])
        for t3 in range(3):
            op("v", lambda E, bank=bank, off=off, t3=t3, c=c, ar=ar:
               E.scalar_tensor_tensor(ar[:, c * 128:(c + 1) * 128],
                                      bank[:, off + t3 * 128:off + (t3 + 1) * 128],
                                      g3[:, t3:t3 + 1], ar[:, c * 128:(c + 1) * 128],
                                      ALU.mult, ALU.add),
               ins=[g3, ar], outs=[ar])
        ev = inc("v", s_v)
        slot_free[_nm(bank[:, 0:1])] = [(s_v, ev)]
        if skey:
            slot_free[skey] = [(s_v, ev)]
        return ev

    def wide_grp(stat_h, stat_l, c, wh, wl, pdst, start, stop, hi_wait=(), lo_wait=(),
                 inc_last=None, h0_first=False):
        for sem, val in hi_wait:
            wait("t", sem, val)
        Sh, Sl = stat_h[:, c, :], stat_l[:, c, :]
        if h0_first:
            mm(pdst[:, H0], Sh, wh[:, c, H0], start, False)
            mm(pdst[:, H0], Sh, wl[:, c, H0], False, False)
            for sem, val in lo_wait:
                wait("t", sem, val)
            ev_h0 = mm(pdst[:, H0], Sl, wh[:, c, H0], False, stop, inc=inc_last)
            mm(pdst[:, H1], Sh, wh[:, c, H1], start, False)
            mm(pdst[:, H1], Sh, wl[:, c, H1], False, False)
            return ev_h0, mm(pdst[:, H1], Sl, wh[:, c, H1], False, stop, inc=inc_last)
        mm(pdst[:, H0], Sh, wh[:, c, H0], start, False)
        mm(pdst[:, H1], Sh, wh[:, c, H1], start, False)
        mm(pdst[:, H0], Sh, wl[:, c, H0], False, False)
        mm(pdst[:, H1], Sh, wl[:, c, H1], False, False)
        for sem, val in lo_wait:
            wait("t", sem, val)
        ev_h0 = mm(pdst[:, H0], Sl, wh[:, c, H0], False, stop, inc=inc_last)
        return ev_h0, mm(pdst[:, H1], Sl, wh[:, c, H1], False, stop, inc=inc_last)


    wait("v", s_t, ev_p1_gram[n_tiles - 1])
    op("v", lambda E: E.tensor_tensor(outbuf[:, :], ps_out[:, :], u0[:, :], ALU.mult),
       ins=[u0, outbuf], outs=[outbuf])
    op("v", lambda E: E.reduce_sum(negmu[:, 0:1], outbuf[:, :], axis=AX.X),
       ins=[outbuf], outs=[negmu])
    ev_red = inc("v", s_v)
    slot_free["out:h0"] = [(s_v, ev_red)]
    slot_free["out:h1"] = [(s_v, ev_red)]
    wait("t", s_v, ev_red)
    op("t", lambda E: E.transpose(ps_trF[0:1, 0:128], negmu[:, 0:1], eye32[:, :]))
    ev_tt = inc("t", s_t)
    wait("v", s_t, ev_tt)
    op("v", lambda E: E.reduce_sum(sca[0:1, 1:2], ps_trF[0:1, 0:128], axis=AX.X),
       ins=[sca], outs=[sca])
    op("v", lambda E: E.tensor_scalar(sca[0:1, 2:3], sca[0:1, 1:2], 1.0 / float(S_loc),
                                      None, ALU.mult), ins=[sca], outs=[sca])
    ev_v1 = inc("v", s_v)
    wait("s", s_v, ev_v1)
    op("s", lambda E: E.activation(sca[0:1, 3:4], sca[0:1, 0:1], ACT.Ln), ins=[sca], outs=[sca])
    ev_s1 = inc("s", s_s)
    wait("v", s_s, ev_s1)
    op("v", lambda E: E.reciprocal(sca[0:1, 4:5], sca[0:1, 3:4]), ins=[sca], outs=[sca])
    op("v", lambda E: E.tensor_scalar(sca[0:1, 5:6], sca[0:1, 2:3], 10.0, None, ALU.mult),
       ins=[sca], outs=[sca])
    op("v", lambda E: E.tensor_tensor(sca[0:1, 6:7], sca[0:1, 5:6], sca[0:1, 4:5], ALU.mult),
       ins=[sca], outs=[sca])
    op("v", lambda E: E.tensor_scalar(sca[0:1, 7:8], sca[0:1, 6:7], -1.0, CLIP,
                                      ALU.mult, ALU.min), ins=[sca], outs=[sca])
    op("v", lambda E: E.tensor_scalar(sca[0:1, 7:8], sca[0:1, 7:8], -CLIP, None, ALU.max),
       ins=[sca], outs=[sca])
    ev_v2 = inc("v", s_v)
    wait("s", s_v, ev_v2)
    op("s", lambda E: E.activation(sca[0:1, 8:9], sca[0:1, 7:8], ACT.Exp), ins=[sca], outs=[sca])
    ev_s2 = inc("s", s_s)
    wait("v", s_s, ev_s2)
    op("v", lambda E: E.tensor_scalar(sca[0:1, 9:10], sca[0:1, 8:9], 1.0, None, ALU.add),
       ins=[sca], outs=[sca])
    op("v", lambda E: E.reciprocal(sca[0:1, 10:11], sca[0:1, 9:10]), ins=[sca], outs=[sca])
    op("v", lambda E: E.tensor_scalar(ccv[:, :], oh_sb[:, :], sca[0:1, 10:11], None, ALU.mult),
       ins=[oh_sb, sca], outs=[ccv])
    ev_ccv = inc("v", s_v)
    wait("g", s_v, ev_ccv)
    dma("g", cc_in[:, :].opt(), ccv[:, :], s_c, outs=[])
    b.wait("g", s_c, b.cnt[s_c])
    op("g", lambda E: E.collective_compute("AllReduce", ALU.add,
                                           replica_groups=[list(range(B))],
                                           ins=[cc_in.ap().opt()], outs=[cc_out.ap().opt()]),
       inc=s_cc, amt=1)
    b.wait("g", s_cc, b.cnt[s_cc])
    dma("g", alldwc[:, :], cc_out[:, :].opt(), s_c)
    ev_adwc = b.cnt[s_c]
    p2_load(0)
    p2_load(1)
    p2_preamble(0)
    halves_nxt = tr_phase(arena[0][:, :], xnh_fm[:, :], xnl_fm[:, :],
                          src_wait=[(s_v, ev_p2_xn[0])], hi_eng="v")

    wait("v", s_c, ev_adwc)
    op("v", lambda E: E.reduce_sum(sca[0:1, 11:12], alldwc[:, :], axis=AX.X),
       ins=[alldwc, sca], outs=[sca])
    op("v", lambda E: E.tensor_scalar(sca[0:1, 12:13], sca[0:1, 11:12], 0.125, None, ALU.mult),
       ins=[sca], outs=[sca])
    op("v", lambda E: E.tensor_scalar(sca[0:1, 13:14], sca[0:1, 12:13], 0.99, None, ALU.is_gt),
       ins=[sca], outs=[sca])
    op("v", lambda E: E.tensor_scalar(sca[0:1, 14:15], sca[0:1, 12:13], 0.01, None, ALU.is_lt),
       ins=[sca], outs=[sca])
    op("v", lambda E: E.tensor_scalar(sca[0:1, 15:16], sca[0:1, 13:14], -1.0, 1.0,
                                      ALU.mult, ALU.add), ins=[sca], outs=[sca])
    op("v", lambda E: E.tensor_scalar(sca[0:1, 16:17], sca[0:1, 14:15], -1.0, 1.0,
                                      ALU.mult, ALU.add), ins=[sca], outs=[sca])
    op("v", lambda E: E.tensor_tensor(sca[0:1, 17:18], sca[0:1, 15:16], sca[0:1, 16:17],
                                      ALU.mult), ins=[sca], outs=[sca])
    op("v", lambda E: E.tensor_tensor(sca[0:1, 18:19], sca[0:1, 17:18], sca[0:1, 10:11],
                                      ALU.mult), ins=[sca], outs=[sca])
    op("v", lambda E: E.tensor_tensor(sca[0:1, 19:20], sca[0:1, 18:19], sca[0:1, 13:14],
                                      ALU.add), ins=[sca], outs=[sca])
    op("v", lambda E: E.tensor_scalar(sca[0:1, 20:21], sca[0:1, 19:20], 0.1, None, ALU.mult),
       ins=[sca], outs=[sca])
    op("v", lambda E: E.tensor_scalar(sca[0:1, 21:22], sca[0:1, 19:20], -1.0, 1.0,
                                      ALU.mult, ALU.add), ins=[sca], outs=[sca])
    ev_coef = inc("v", s_v)
    st["out_rd"] = ev_coef
    wait("g", s_v, ev_coef)
    dma("g", scr2[:, :].opt(), sca[0:1, 20:22], s_c, outs=[])
    b.wait("g", s_c, b.cnt[s_c])
    dma("g", c01b[:, :], scr2.ap()[0:1, 0:1].partition_broadcast(128), s_c)
    dma("g", omcb[:, :], scr2.ap()[0:1, 1:2].partition_broadcast(128), s_c)
    ev_bc = b.cnt[s_c]

    def tail_post(p):
        i0, ar0 = p["i"], p["ar"]
        wait("v", s_s, p["ev_exp"])
        wait("v", s_t, p["ev_wo_t"])
        if i0 == 0:
            wait("v", s_c, TK["bao"])
        op("v", lambda E, i0=i0: E.scalar_tensor_tensor(outbuf[:, :], ps_out[:, :], 1.0,
                                                        xbuf[:, i0 % 2, :], ALU.mult, ALU.add),
           ins=[xbuf, outbuf], outs=[outbuf])
        op("v", lambda E, ar0=ar0: E.tensor_tensor(outbuf[:, :], outbuf[:, :], ar0[:, :],
                                                   ALU.add), ins=[outbuf, ar0], outs=[outbuf])
        op("v", lambda E: E.tensor_tensor(outbuf[:, :], outbuf[:, :], bao_rep[:, :],
                                          ALU.add), ins=[outbuf, bao_rep], outs=[outbuf])
        ev_fin = inc("v", s_v)
        st["out_rd"] = ev_fin
        st["xfree"][i0] = ev_fin
        wait("g", s_v, ev_fin)
        dma("g", out_hbm[i0 * 128:(i0 + 1) * 128, :], outbuf[:, :], s_o, outs=[])
        st["store"][i0] = b.cnt[s_o]
        p2_load(i0 + 2)

    prev = None
    for i in range(n_tiles):
        ar = arena[i % 2]
        halves = halves_nxt
        for c in range(NBLK):
            ev_ssm, ev_f1h0, ev_grp = grp_ssm_ffn1(i, c, halves[c // 4])
            blend(i, c, ev_ssm)
        ev_ffn1 = b.cnt[s_t]

        wait("s", s_t, ev_f1h0)
        op("s", lambda E: E.copy(u0[:, 0:512], ps_h[:, 0:512]), ins=[u0], outs=[u0])
        ev_h32a = inc("s", s_s)
        wait("s", s_t, ev_ffn1)
        op("s", lambda E: E.copy(u0[:, 512:1024], ps_h[:, 512:1024]), ins=[u0], outs=[u0])
        ev_h32 = inc("s", s_s)
        h1_halves = tr_phase(u0[:, :], h1fm_h, h1fm_l, src_wait=[(s_s, ev_h32a)],
                             src_wait2=[(s_s, ev_h32)])

        for c in range(NBLK - 1):
            wide_grp(h1h, h1l, c, fwh[1], fwl[1], ps_h, c == 0, False,
                     hi_wait=h1_halves[c // 4][0], lo_wait=h1_halves[c // 4][1])

        if prev is not None:
            tail_post(prev)

        ln_stats(ar)
        op("v", lambda E, ar=ar: E.tensor_scalar(ar[:, :], ar[:, :], negmu[:, 0:1],
                                                 rstd[:, 0:1], ALU.add, ALU.mult),
           ins=[ar, negmu, rstd], outs=[ar])
        ev_norm = inc("v", s_v)

        n_halves = tr_phase(ar[:, :], nfm_h, nfm_l, src_wait=[(s_v, ev_norm)])
        ev_ffn2 = wide_grp(h1h, h1l, NBLK - 1, fwh[1], fwl[1], ps_h, False, True,
                           hi_wait=h1_halves[1][0], lo_wait=h1_halves[1][1],
                           inc_last=s_t, h0_first=True)
        wait("s", s_t, ev_ffn2[0])
        op("s", lambda E: E.copy(u0[:, 0:512], ps_h[:, 0:512]), ins=[u0], outs=[u0])
        ev_h32c = inc("s", s_s)
        wait("s", s_t, ev_ffn2[1])
        op("s", lambda E: E.copy(u0[:, 512:1024], ps_h[:, 512:1024]), ins=[u0], outs=[u0])
        ev_h32b = inc("s", s_s)

        p2_preamble(i + 1)

        for c in range(NBLK - 1):
            ew = list(n_halves[c // 4][0])
            if c == 0:
                ew.append((s_v, st["out_rd"]))
                ew += slot_free.get(_nm(ps_out[:, 0:1]), [])
            wide_grp(nfh, nfl, c, woh, wol, ps_out, c == 0, False,
                     hi_wait=ew, lo_wait=n_halves[c // 4][1])

        h2_halves = tr_phase(u0[:, :], h2fm_h, h2fm_l, src_wait=[(s_s, ev_h32c)],
                             src_wait2=[(s_s, ev_h32b)])
        ev_wo_t = wide_grp(nfh, nfl, NBLK - 1, woh, wol, ps_out, False, True,
                           hi_wait=n_halves[1][0], lo_wait=n_halves[1][1],
                           inc_last=s_t)[1]

        for c in range(2):
            wide_grp(h2h, h2l, c, fwh[2], fwl[2], ps_h, c == 0, False,
                     hi_wait=h2_halves[c // 4][0], lo_wait=h2_halves[c // 4][1])
        if i + 1 < n_tiles:
            halves_nxt = tr_phase(arena[(i + 1) % 2][:, :], xnh_fm[:, :], xnl_fm[:, :],
                                  src_wait=[(s_v, ev_p2_xn[i + 1])], hi_eng="v")
        for c in range(2, NBLK):
            ev_ffn3 = wide_grp(h2h, h2l, c, fwh[2], fwl[2], ps_h, False, c == NBLK - 1,
                               hi_wait=h2_halves[c // 4][0], lo_wait=h2_halves[c // 4][1],
                               inc_last=s_t if c == NBLK - 1 else None,
                               h0_first=(c == NBLK - 1))

        op("s", lambda E: E.activation(sca[0:1, 30:31], sca[0:1, 30:31], ACT.Tanh),
           ins=[sca], outs=[sca])
        wait("v", s_t, ev_ffn3[0])
        if i == 0:
            wait("v", s_w, ev_w0)
        wait("v", s_o, st["store"][i - 1] if i >= 1 else 0)
        op("v", lambda E: E.tensor_tensor(outbuf[:, H0], ps_h[:, H0], fb_rep[:, H0],
                                          ALU.add), ins=[fb_rep, outbuf], outs=[outbuf])
        ev_fb0 = inc("v", s_v)
        st["psh_rd0"] = ev_fb0
        wait("v", s_t, ev_ffn3[1])
        op("v", lambda E: E.tensor_tensor(outbuf[:, H1], ps_h[:, H1], fb_rep[:, H1],
                                          ALU.add), ins=[fb_rep, outbuf], outs=[outbuf])
        ev_fb = inc("v", s_v)
        st["psh_rd"] = ev_fb
        wait("s", s_v, ev_fb)
        op("s", lambda E, ar=ar: E.activation(ar[:, :], outbuf[:, :], ACT.Tanh),
           ins=[outbuf], outs=[ar])
        ev_exp = inc("s", s_s)
        op("s", lambda E: E.activation(sca[0:1, 31:32], epst[0:1, 0:1], ACT.Ln,
                                       bias=epst[0:1, 0:1]), ins=[sca, epst], outs=[sca])
        prev = {"i": i, "ar": ar, "ev_wo_t": ev_wo_t, "ev_exp": ev_exp}

    tail_post(prev)
    b.wait("g", s_o, b.cnt[s_o])

    block = es.enter_context(nc.Block())
    with es:
        @block.sync
        def _(E):
            for f in b.SCH["y"]:
                f(E)

        @block.vector
        def _(E):
            for f in b.SCH["v"]:
                f(E)

        @block.scalar
        def _(E):
            for f in b.SCH["s"]:
                f(E)

        @block.tensor
        def _(E):
            for f in b.SCH["t"]:
                f(E)

        @block.gpsimd
        def _(E):
            for f in b.SCH["g"]:
                f(E)
    return nc


_CACHE = {}


def _get_nc(n_tiles):
    if n_tiles not in _CACHE:
        _CACHE[n_tiles] = build_nc(n_tiles)
    return _CACHE[n_tiles]


def kernel(L0_attn_x, L0_attn_W_q, L0_attn_W_k, L0_attn_W_v, L0_attn_W_o,
           L0_attn_SSM_A, L0_attn_SSM_B, L0_attn_SSM_C, L0_attn_ln_gamma,
           L0_attn_ln_beta, L0_ffn_weight, L0_ffn_aux_weight1, L0_ffn_aux_weight2,
           L0_ffn_bias):
    x = np.ascontiguousarray(np.asarray(L0_attn_x, dtype=np.float32))
    Bsz, S_, D_ = x.shape
    assert D_ == D and Bsz == B
    n_tiles = S_ // 128
    idx = np.arange(NBLK)

    def diag(M):
        M = np.asarray(M, dtype=np.float32)
        return np.ascontiguousarray(M.reshape(NBLK, 128, NBLK, 128)[idx, :, idx, :])

    common = {
        "wq": np.ascontiguousarray(np.asarray(L0_attn_W_q, dtype=np.float32)),
        "wk": np.ascontiguousarray(np.asarray(L0_attn_W_k, dtype=np.float32)),
        "wv": np.ascontiguousarray(np.asarray(L0_attn_W_v, dtype=np.float32)),
        "ad": diag(L0_attn_SSM_A),
        "bd": diag(L0_attn_SSM_B),
        "cd": diag(L0_attn_SSM_C),
        "wo": np.ascontiguousarray(np.asarray(L0_attn_W_o, dtype=np.float32)),
        "gamma": np.ascontiguousarray(np.asarray(L0_attn_ln_gamma, dtype=np.float32)),
        "beta": np.ascontiguousarray(np.asarray(L0_attn_ln_beta, dtype=np.float32)),
        "f1": np.ascontiguousarray(np.asarray(L0_ffn_weight, dtype=np.float32)),
        "f2": np.ascontiguousarray(np.asarray(L0_ffn_aux_weight1, dtype=np.float32)),
        "f3": np.ascontiguousarray(np.asarray(L0_ffn_aux_weight2, dtype=np.float32)),
        "fb": np.ascontiguousarray(np.asarray(L0_ffn_bias, dtype=np.float32)),
        "eye": np.eye(128, dtype=np.float32),
    }
    eyeB = np.eye(B, dtype=np.float32)
    in_maps = [dict(common, x=np.ascontiguousarray(x[bb]),
                    onehot=np.ascontiguousarray(eyeB[bb:bb + 1]))
               for bb in range(B)]
    nc = _get_nc(n_tiles)
    res = run_bass_kernel_spmd(nc, in_maps, core_ids=list(range(B)))
    return np.stack([np.asarray(r["out"]) for r in res.results], axis=0)


# version 37
# speedup vs baseline: 1.0136x; 1.0078x over previous
import contextlib
import numpy as np

import concourse.bass as bass
import concourse.mybir as mybir
from concourse.bass_utils import run_bass_kernel_spmd

F32 = mybir.dt.float32
BF16 = mybir.dt.bfloat16
AX = mybir.AxisListType
ALU = mybir.AluOpType
ACT = mybir.ActivationFunctionType

B = 8
D = 1024
NBLK = 8
EPS = 1e-5
CLIP = 88.0
H0 = slice(0, 512)
H1 = slice(512, 1024)


def _nm(ap):
    n = ap.name
    return n() if callable(n) else n


class Bld:

    def __init__(self):
        self.SCH = {k: [] for k in ("t", "v", "s", "y", "g")}
        self.cnt = {}
        self.dirty = {k: set() for k in self.SCH}
        self.wm = {}

    def op(self, k, f, ins=(), outs=(), inc=None, amt=1):
        nm = {_nm(a) for a in ins} | {_nm(a) for a in outs}
        if k != "t" and (nm & self.dirty[k]):
            self.SCH[k].append(lambda E: E.drain())
            self.dirty[k].clear()
        if inc is not None:
            self.cnt[inc] = self.cnt.get(inc, 0) + amt
            self.SCH[k].append(lambda E, f=f, inc=inc, amt=amt: f(E).then_inc(inc, amt))
            ret = self.cnt[inc]
        else:
            self.SCH[k].append(lambda E, f=f: f(E))
            ret = None
        self.dirty[k] |= {_nm(a) for a in outs}
        return ret

    def wait(self, k, sem, val):
        if not val:
            return
        assert val <= self.cnt.get(sem, 0), f"wait beyond scheduled ({val} > {self.cnt.get(sem, 0)})"
        wm = self.wm.setdefault(k, {})
        if wm.get(sem, -1) >= val:
            return
        wm[sem] = val
        self.SCH[k].append(lambda E, sem=sem, val=val: E.wait_ge(sem, val))

    def inc(self, k, sem, amt=1):
        self.cnt[sem] = self.cnt.get(sem, 0) + amt
        self.SCH[k].append(lambda E, sem=sem, amt=amt: E.drain().then_inc(sem, amt))
        self.dirty[k].clear()
        return self.cnt[sem]


def build_nc(n_tiles):
    nc = bass.Bass("TRN2", target_bir_lowering=False, debug=False)
    S_loc = n_tiles * 128
    assert n_tiles % 2 == 0 and n_tiles >= 4

    x_hbm = nc.dram_tensor("x", [S_loc, D], F32, kind="ExternalInput").ap()
    w_hbm = {}
    for name, shp in (("wq", [NBLK, 128, 128]), ("wk", [NBLK, 128, 128]),
                      ("wv", [NBLK, 128, 128]), ("ad", [NBLK, 128, 128]),
                      ("bd", [NBLK, 128, 128]), ("cd", [NBLK, 128, 128]),
                      ("wo", [D, D]), ("gamma", [D]), ("beta", [D]),
                      ("f1", [D, D]), ("f2", [D, D]), ("f3", [D, D]), ("fb", [D]),
                      ("eye", [128, 128]), ("onehot", [1, B])):
        w_hbm[name] = nc.dram_tensor(name, shp, F32, kind="ExternalInput").ap()
    out_hbm = nc.dram_tensor("out", [S_loc, D], F32, kind="ExternalOutput").ap()
    cc_in = nc.dram_tensor("cc_in", [1, B], F32)
    cc_out = nc.dram_tensor("cc_out", [1, B], F32)
    scr2 = nc.dram_tensor("scr2", [1, 2], F32)
    scrb = nc.dram_tensor("scrb", [1, D], F32)

    es = contextlib.ExitStack()
    sb = lambda n, s, d=F32: es.enter_context(nc.sbuf_tensor(n, s, d))

    gh = sb("gh", [128, NBLK, 387], BF16)
    gl = sb("gl", [128, NBLK, 387], BF16)
    woh = sb("woh", [128, NBLK, D], BF16)
    wol = sb("wol", [128, NBLK, D], BF16)
    fwh = [sb(f"f{j}h", [128, NBLK, D], BF16) for j in (1, 2, 3)]
    fwl = [sb(f"f{j}l", [128, NBLK, D], BF16) for j in (1, 2, 3)]
    bao_rep = sb("bao_rep", [128, D], F32)
    fb_rep = sb("fb_rep", [128, D], F32)
    bao_row = sb("bao_row", [1, D], F32)
    ones_row = sb("ones_row", [1, 128], BF16)
    eye32 = sb("eye32", [128, 128], F32)
    gm = sb("gm", [128, NBLK], F32)
    bm32 = sb("bm32", [128, NBLK], F32)
    bmh = sb("bmh", [128, NBLK], BF16)
    bml = sb("bml", [128, NBLK], BF16)

    xbuf = sb("xbuf", [128, 2, D], F32)
    xn = sb("xn", [128, D], F32)
    wstage = sb("wstage", [128, D], F32)
    fstage = sb("fstage", [128, D], F32)
    xnh_fm = sb("xnh_fm", [128, D], BF16)
    xnl_fm = sb("xnl_fm", [128, D], BF16)
    u0 = sb("u0", [128, D], F32)
    u4k_0 = sb("u4k_0", [128, 2048], BF16)
    u4k_1 = sb("u4k_1", [128, 2048], BF16)
    u4k_2 = sb("u4k_2", [128, 2048], BF16)
    outbuf = sb("outbuf", [128, D], F32)
    stats6 = sb("stats6", [128, 2, 2, 6], F32)
    mv = sb("mv", [128, 2, 2], F32)
    negmu = sb("negmu", [128, 2], F32)
    la = sb("la", [128, 2], F32)
    rstd = sb("rstd", [128, 2], F32)
    nmr = sb("nmr", [128, 2], F32)
    epst = sb("epst", [128, 1], F32)
    nhc = sb("nhc", [128, 1], F32)
    g3 = sb("g3", [128, 3], F32)
    com2 = sb("com2", [128, 2], F32)
    c01b = com2[:, 0:1]
    omcb = com2[:, 1:2]
    sca = sb("sca", [1, 32], F32)
    oh_sb = sb("oh_sb", [1, B], F32)
    ccv = sb("ccv", [1, B], F32)
    alldwc = sb("alldwc", [1, B], F32)

    stg32b = sb("stg32b", [128, 771], F32)
    stg32b2 = sb("stg32b2", [128, D], F32)
    stg32_p = [wstage[:, 0:771], stg32b[:, :]]
    stgh_p = [u4k_1[:, 0:771], u4k_2[:, 0:771]]
    stgl_p = [u4k_1[:, 1024:1795], u4k_2[:, 1024:1795]]
    wraw_h_p = [u4k_2[:, 0:1024], u4k_1[:, 0:1024]]
    wraw_l_p = [u4k_2[:, 1024:2048], u4k_1[:, 1024:2048]]
    p1set = None
    nfm_h = u4k_1[:, 0:1024]
    nfm_l = u4k_1[:, 1024:2048]
    h1fm_h = u4k_2[:, 0:1024]
    h1fm_l = u4k_2[:, 1024:2048]
    h2fm_h = u4k_0[:, 0:1024]
    h2fm_l = u4k_0[:, 1024:2048]
    m_view = u0[:, :].rearrange("p (n d) -> p n d", n=NBLK)
    p1set = [(u4k_0[:, 0:1024], u4k_0[:, 1024:2048]),
             (xnh_fm[:, :], xnl_fm[:, :])]

    ps_trF = es.enter_context(nc.psum_tensor("ps_trF", [128, 1024], F32))
    ps_ssm = [es.enter_context(nc.psum_tensor(f"ps_ssm{i}", [128, 512], F32)) for i in range(2)]
    ps_out = es.enter_context(nc.psum_tensor("ps_out", [128, 1024], F32))
    ps_h = es.enter_context(nc.psum_tensor("ps_h", [128, 1024], F32))

    s_x = es.enter_context(nc.semaphore("s_x"))
    s_w = es.enter_context(nc.semaphore("s_w"))
    s_o = es.enter_context(nc.semaphore("s_o"))
    s_c = es.enter_context(nc.semaphore("s_c"))
    s_cc = es.enter_context(nc.semaphore("s_cc"))
    s_t = es.enter_context(nc.semaphore("s_t"))
    s_v = es.enter_context(nc.semaphore("s_v"))
    s_s = es.enter_context(nc.semaphore("s_s"))
    s_g = es.enter_context(nc.semaphore("s_g"))

    b = Bld()
    op, wait, inc = b.op, b.wait, b.inc

    def dma(k, dst, src, sem, outs=None, slow=False, serialize=True):
        if serialize:
            wait(k, sem, b.cnt.get(sem, 0))
        return op(k, lambda E, dst=dst, src=src, slow=slow:
                  E.dma_start(out=dst, in_=src, allow_slow_non_contiguous=slow),
                  outs=outs if outs is not None else [dst], inc=sem, amt=16)

    def mm(pdst, lhsT, rhs, start, stop, inc=None):
        return op("t", lambda E, p=pdst, a=lhsT, bb=rhs, s1=start, s2=stop:
                  E.matmul(p, a, bb, start=s1, stop=s2), inc=inc)

    def mm3(pdst, ah, al, bh, bl, start, stop, inc=None):
        mm(pdst, ah, bh, start, False)
        mm(pdst, ah, bl, False, False)
        return mm(pdst, al, bh, False, stop, inc=inc)

    def split_s(hi_dst, src, ins=()):
        op("s", lambda E, hi_dst=hi_dst, src=src: E.copy(hi_dst, src),
           ins=list(ins) + [src], outs=[hi_dst])

    def split_v(lo_dst, src, hi, ins=()):
        op("v", lambda E, lo_dst=lo_dst, src=src, hi=hi:
           E.scalar_tensor_tensor(lo_dst, src, 1.0, hi, ALU.mult, ALU.subtract),
           ins=list(ins) + [src, hi], outs=[lo_dst])

    def ln_stats(src_ap, p=0):
        for c2 in range(2):
            op("v", lambda E, c2=c2, src_ap=src_ap, p=p:
               E.bn_stats(stats6[:, p, c2, :], src_ap[:, c2 * 512:(c2 + 1) * 512]),
               ins=[src_ap], outs=[stats6])
        op("v", lambda E, p=p: E.bn_aggr(mv[:, p, :],
                                         stats6[:, p, :, :].rearrange("p c s -> p (c s)")),
           ins=[stats6], outs=[mv])
        op("v", lambda E, p=p: E.tensor_scalar(negmu[:, p:p + 1], mv[:, p, 0:1], -1.0,
                                               None, ALU.mult), ins=[mv], outs=[negmu])
        ev = inc("v", s_v)
        wait("s", s_v, ev)
        op("s", lambda E, p=p: E.activation(la[:, p:p + 1], mv[:, p, 1:2], ACT.Ln,
                                            bias=epst[:, 0:1]), ins=[mv, epst], outs=[la])
        op("s", lambda E, p=p: E.activation(rstd[:, p:p + 1], la[:, p:p + 1], ACT.Exp,
                                            scale=nhc[:, 0:1]), ins=[la, nhc], outs=[rstd])
        ev_s = inc("s", s_s)
        wait("v", s_s, ev_s)

    slot_free = {}

    def slot_wait(k, p):
        for sem, val in slot_free.get(_nm(p[:, 0:1]), []):
            wait(k, sem, val)

    def tr_phase(src32_ap, hi_dst, lo_dst, src_wait=(), src_wait2=(), hi_eng="s"):
        slot_wait("t", ps_trF)
        for sem, val in src_wait:
            wait("t", sem, val)
        halves = []
        quarters = []
        for half in range(2):
            if half == 1:
                for sem, val in src_wait2:
                    wait("t", sem, val)
            ev_t = None
            for c in range(4 * half, 4 * half + 4):
                ev_t = op("t", lambda E, c=c, src32_ap=src32_ap:
                          E.transpose(ps_trF[:, c * 128:(c + 1) * 128],
                                      src32_ap[:, c * 128:(c + 1) * 128], eye32[:, :]),
                          inc=s_t if c % 4 == 3 else None)
            hs = slice(half * 512, (half + 1) * 512)
            if hi_eng == "s":
                wait("s", s_t, ev_t)
                qa = slice(half * 512, half * 512 + 256)
                split_s(hi_dst[:, qa], ps_trF[:, qa])
                q0 = (s_s, inc("s", s_s))
                qb = slice(half * 512 + 256, (half + 1) * 512)
                split_s(hi_dst[:, qb], ps_trF[:, qb])
                q1 = (s_s, inc("s", s_s))
                quarters += [q0, q0, q1, q1]
                hi_ev = q1
            else:
                wait("v", s_t, ev_t)
                op("v", lambda E, hs=hs, hi_dst=hi_dst:
                   E.tensor_copy(hi_dst[:, hs], ps_trF[:, hs]), outs=[hi_dst])
                hi_ev = (s_v, inc("v", s_v))
            wait("v", s_t, ev_t)
            wait("v", hi_ev[0], hi_ev[1])
            split_v(lo_dst[:, hs], ps_trF[:, hs], hi_dst[:, hs])
            ve = inc("v", s_v)
            halves.append(([hi_ev], [(s_v, ve)]))
        slot_free["trF:h0"] = halves[0][0] + halves[0][1]
        slot_free["trF:h1"] = halves[1][0] + halves[1][1]
        slot_free[_nm(ps_trF[:, 0:1])] = (halves[0][0] + halves[0][1]
                                          + halves[1][0] + halves[1][1])
        return halves, quarters

    dma("y", eye32[:, :], w_hbm["eye"][:, :], s_w)
    dma("y", gm[:, :], w_hbm["gamma"].rearrange("(c d) -> d c", d=128), s_w, slow=True)
    dma("y", bm32[:, :], w_hbm["beta"].rearrange("(c d) -> d c", d=128), s_w, slow=True)
    dma("y", fb_rep[:, :], w_hbm["fb"].unsqueeze(0).partition_broadcast(128), s_w)
    dma("y", oh_sb[:, :], w_hbm["onehot"][:, :], s_w)
    ev_w0 = b.cnt[s_w]

    for f in (lambda E: E.memset(epst[:, :], EPS),
              lambda E: E.memset(nhc[:, :], -0.5),
              lambda E: E.memset(ones_row[:, :], 1.0),
              lambda E: E.memset(sca[:, :], 0.0)):
        op("v", f, outs=[epst, nhc, ones_row, sca])
    for stg in stg32_p:
        for col in (256, 513, 770):
            op("v", lambda E, stg=stg, col=col: E.memset(stg[:, col:col + 1], 1.0),
               ins=[stg], outs=[stg])
    ev_ones0 = inc("v", s_v)
    op("v", lambda E: E.memset(sca[0:1, 0:1], float(S_loc) + 1.0), ins=[sca], outs=[sca])

    ev_s0 = inc("s", s_s)
    wait("v", s_w, ev_w0)
    ev_v0 = inc("v", s_v)

    arena = [xn, wstage]
    p1arena = [xn, outbuf]

    ev_p1_split = [None] * n_tiles
    ev_p1_xrd = [None] * n_tiles
    ev_p1_gram = [None] * n_tiles

    def p1_load(i):
        if i >= 2:
            wait("y", s_s, ev_p1_xrd[i - 2])
        dma("y", xbuf[:, i % 2, :], x_hbm[i * 128:(i + 1) * 128, :], s_x, outs=[xbuf])

    def p1_preamble(i):
        p = i % 2
        wait("v", s_x, 16 * (i + 1))
        for c2 in range(2):
            op("v", lambda E, c2=c2, i=i, p=p:
               E.bn_stats(stats6[:, p, c2, :], xbuf[:, i % 2, c2 * 512:(c2 + 1) * 512]),
               ins=[xbuf], outs=[stats6])
        op("v", lambda E, p=p: E.bn_aggr(mv[:, p, :],
                                         stats6[:, p, :, :].rearrange("p c s -> p (c s)")),
           ins=[stats6], outs=[mv])
        op("v", lambda E, p=p: E.tensor_scalar(negmu[:, p:p + 1], mv[:, p, 0:1], -1.0,
                                               None, ALU.mult), ins=[mv], outs=[negmu])
        ev_v = inc("v", s_v)
        wait("s", s_v, ev_v)
        op("s", lambda E, p=p: E.activation(la[:, p:p + 1], mv[:, p, 1:2], ACT.Ln,
                                            bias=epst[:, 0:1]), ins=[mv, epst], outs=[la])
        op("s", lambda E, p=p: E.activation(rstd[:, p:p + 1], la[:, p:p + 1], ACT.Exp,
                                            scale=nhc[:, 0:1]), ins=[la, nhc], outs=[rstd])
        op("s", lambda E, p=p: E.activation(nmr[:, p:p + 1], negmu[:, p:p + 1],
                                            ACT.Identity, scale=rstd[:, p:p + 1]),
           ins=[negmu, rstd], outs=[nmr])
        ar = p1arena[i % 2]
        if i >= 2:
            wait("s", s_t, ev_p1_gram[i - 2])
        op("s", lambda E, i=i, ar=ar, p=p: E.activation(ar[:, :], xbuf[:, i % 2, :],
                                                        ACT.Identity,
                                                        bias=nmr[:, p:p + 1],
                                                        scale=rstd[:, p:p + 1]),
           ins=[xbuf, nmr, rstd], outs=[ar])
        ev_xn = inc("s", s_s)
        ev_p1_xrd[i] = ev_xn
        ev_p1_split[i] = ev_xn

    def p1_gram(i):
        ar = p1arena[i % 2]
        wait("t", s_s, ev_p1_split[i])
        for n in range(NBLK):
            csl = slice(n * 128, (n + 1) * 128)
            first = (i == 0) and (n in (0, 4))
            last = (i == n_tiles - 1) and (n in (3, 7))
            ev_p1_gram[i] = mm(ps_out[:, csl], ar[:, csl], ar[:, csl], first, last,
                               inc=s_t if n == NBLK - 1 else None)

    TK = {"split": (ev_s0, ev_v0), "evac_s": None, "evac_v": None, "gmm": None,
          "bias_t": None, "wo": None, "f": None, "bao": None, "gmm_final": None,
          "fp": {0: None, 1: None}}

    def g_task(n):
        stg32 = stg32_p[n % 2]
        wait("y", s_v, ev_ones0)
        if TK["gmm"] is not None:
            wait("y", s_t, TK["gmm"])
        wait("y", s_w, b.cnt.get(s_w, 0))
        for dst0, srcn in ((0, "wq"), (128, "ad"), (257, "wk"), (385, "bd"),
                           (514, "wv"), (642, "cd")):
            dma("y", stg32[:, dst0:dst0 + 128], w_hbm[srcn][n], s_w, outs=[stg32],
                serialize=False)
        ev_ld = b.cnt[s_w]
        return lambda: g_back(n, ev_ld)

    def g_back(n, ev_ld):
        stg32 = stg32_p[n % 2]
        wait("t", s_w, ev_ld)
        wait("t", s_v, ev_ones0)
        if TK["evac_s"] is not None:
            wait("t", s_s, TK["evac_s"])
            wait("t", s_v, TK["evac_v"])
        mm(ps_ssm[0][:, 0:257], stg32[:, 0:128], stg32[:, 128:385], True, True)
        mm(ps_ssm[1][:, 0:129], stg32[:, 257:385], stg32[:, 385:514], True, True)
        ev_gmm = TK["gmm"] = mm(ps_ssm[1][:, 129:258], stg32[:, 514:642],
                                stg32[:, 642:771], True, True, inc=s_t)
        evmap = ((0, ps_ssm[0][:, 0:128]), (384, ps_ssm[0][:, 128:129]),
                 (128, ps_ssm[1][:, 0:128]), (385, ps_ssm[1][:, 128:129]),
                 (256, ps_ssm[1][:, 129:257]), (386, ps_ssm[1][:, 257:258]))
        wait("s", s_t, ev_gmm)
        for dc, pslc in evmap:
            w_ = pslc.shape[1]
            op("s", lambda E, n=n, dc=dc, p=pslc, w_=w_: E.copy(gh[:, n, dc:dc + w_], p),
               outs=[gh])
        TK["evac_s"] = inc("s", s_s)
        wait("v", s_t, ev_gmm)
        wait("v", s_s, TK["evac_s"])
        for dc, pslc in evmap:
            w_ = pslc.shape[1]
            op("v", lambda E, n=n, dc=dc, p=pslc, w_=w_:
               E.scalar_tensor_tensor(gl[:, n, dc:dc + w_], p, 1.0, gh[:, n, dc:dc + w_],
                                      ALU.mult, ALU.subtract), ins=[gh], outs=[gl])
        op("v", lambda E, n=n: E.tensor_copy(m_view[:, n, :], ps_ssm[0][:, 129:257]),
           outs=[u0])
        TK["evac_v"] = inc("v", s_v)
        return None

    def wo_task(c):
        if TK["wo"] is None:
            TK["wo"] = (None, TK["split"][1])
            TK["gmm_final"] = TK["gmm"]
        wait("y", s_v, TK["wo"][1])
        wait("y", s_t, TK["gmm_final"])
        dma("y", wstage[:, :], w_hbm["wo"][c * 128:(c + 1) * 128, :], s_w)
        if TK["bias_t"] is not None:
            wait("y", s_t, TK["bias_t"])
        else:
            wait("y", s_t, TK["gmm_final"])
        dma("y", stg32b2[:, :], w_hbm["wo"][c * 128:(c + 1) * 128, :], s_w)
        ev_ld = b.cnt[s_w]
        wait("v", s_w, ev_ld)
        op("v", lambda E, c=c: E.tensor_scalar(wstage[:, :], wstage[:, :], gm[:, c:c + 1],
                                               None, ALU.mult), ins=[wstage, gm], outs=[wstage])
        ev_sc = inc("v", s_v)
        wait("s", s_v, ev_sc)
        op("s", lambda E, c=c: E.copy(woh[:, c, :], wstage[:, :]), ins=[wstage], outs=[woh])
        ev_wh = inc("s", s_s)
        wait("v", s_s, ev_wh)
        op("v", lambda E, c=c: E.scalar_tensor_tensor(wol[:, c, :], wstage[:, :], 1.0,
                                                      woh[:, c, :], ALU.mult, ALU.subtract),
           ins=[wstage, woh], outs=[wol])
        TK["wo"] = (ev_wh, inc("v", s_v))
        return lambda: wo_back(c, ev_ld)

    def wo_back(c, ev_ld):
        wait("t", s_w, ev_ld)
        mm(ps_h[0:1, H0], bm32[:, c:c + 1], stg32b2[:, H0],
           start=(c == 0), stop=(c == NBLK - 1))
        TK["bias_t"] = mm(ps_h[0:1, H1], bm32[:, c:c + 1], stg32b2[:, H1],
                          start=(c == 0), stop=(c == NBLK - 1), inc=s_t)
        return None

    def bias_evac_task():
        TK["f"] = TK["wo"]

        def back():
            wait("v", s_t, TK["bias_t"])
            op("v", lambda E: E.tensor_copy(bao_row[:, :], ps_h[0:1, :]), outs=[bao_row])
            ev_bs = inc("v", s_v)
            wait("g", s_v, ev_bs)
            dma("g", scrb[:, :].opt(), bao_row[:, :], s_c, outs=[])
            dma("g", bao_rep[:, :], scrb.ap()[0:1, :].partition_broadcast(128), s_c)
            TK["bao"] = b.cnt[s_c]
        return back

    fst_p = [wstage, fstage]

    def ffn_task(j, c):
        k = j * NBLK + c
        wst = fst_p[k % 2]
        prev = TK["f"] if k == 0 else TK["fp"][k % 2]
        if prev is not None:
            wait("y", s_s, prev[0])
            wait("y", s_v, prev[1])
        dma("y", wst[:, :], w_hbm[f"f{j + 1}"][c * 128:(c + 1) * 128, :], s_w)
        ev_ld = b.cnt[s_w]
        wait("s", s_w, ev_ld)
        split_s(fwh[j][:, c, :], wst[:, :])
        ev_fh = inc("s", s_s)

        def back():
            wait("v", s_w, ev_ld)
            wait("v", s_s, ev_fh)
            split_v(fwl[j][:, c, :], wst[:, :], fwh[j][:, c, :])
            TK["fp"][k % 2] = (ev_fh, inc("v", s_v))
        return back

    TASKS = ([(lambda n=n: g_task(n)) for n in range(NBLK)]
             + [(lambda c=c: wo_task(c)) for c in range(NBLK)]
             + [bias_evac_task]
             + [(lambda j=j, c=c: ffn_task(j, c)) for j in range(3) for c in range(NBLK)])
    MAXLEAD = [1, 1] + [2] * (NBLK - 2) + [1] * NBLK + [2] + [2] * (3 * NBLK)

    p1_load(0)
    p1_load(1)
    p1_preamble(0)
    p1_preamble(1)
    _ntask = 0
    _nback = 0
    _backs = []
    for i in range(n_tiles):
        p1_gram(i)
        if i + 2 < n_tiles:
            p1_load(i + 2)
            p1_preamble(i + 2)
        tgt_b = i * len(TASKS) // n_tiles
        tgt_f = (i + 1) * len(TASKS) // n_tiles
        while _nback < min(_ntask, tgt_b):
            f = _backs[_nback]
            if f is not None:
                f()
            _nback += 1
        while _ntask < tgt_f:
            while _nback < _ntask + 1 - MAXLEAD[_ntask]:
                f = _backs[_nback]
                if f is not None:
                    f()
                _nback += 1
            _backs.append(TASKS[_ntask]())
            _ntask += 1
    while _nback < len(TASKS):
        f = _backs[_nback]
        if f is not None:
            f()
        _nback += 1



    xfh = xnh_fm[:, :].rearrange("p (n d) -> p n d", n=NBLK)
    xfl = xnl_fm[:, :].rearrange("p (n d) -> p n d", n=NBLK)
    nfh = nfm_h.rearrange("p (n d) -> p n d", n=NBLK)
    nfl = nfm_l.rearrange("p (n d) -> p n d", n=NBLK)
    h1h = h1fm_h.rearrange("p (n d) -> p n d", n=NBLK)
    h1l = h1fm_l.rearrange("p (n d) -> p n d", n=NBLK)
    h2h = h2fm_h.rearrange("p (n d) -> p n d", n=NBLK)
    h2l = h2fm_l.rearrange("p (n d) -> p n d", n=NBLK)

    st = {"psh_rd": None,
          "psh_rd0": None,
          "out_rd": None,
          "store": [0] * (n_tiles + 1),
          "xfree": [None] * (n_tiles + 2)}
    ev_p2_xn = [None] * (n_tiles + 1)

    def p2_load(i):
        if i >= n_tiles:
            return
        if i >= 2:
            wait("y", s_v, st["xfree"][i - 2])
        else:
            wait("y", s_s, ev_p1_xrd[n_tiles - 2 + i])
        dma("y", xbuf[:, i % 2, :], x_hbm[i * 128:(i + 1) * 128, :], s_x, outs=[xbuf])

    def p2_preamble(i):
        if i >= n_tiles:
            return
        wait("v", s_x, 16 * (n_tiles + i + 1))
        ln_stats(xbuf[:, i % 2, :])
        ar = arena[i % 2]
        op("v", lambda E, i=i, ar=ar: E.tensor_scalar(ar[:, :], xbuf[:, i % 2, :],
                                                      negmu[:, 0:1], rstd[:, 0:1],
                                                      ALU.add, ALU.mult),
           ins=[xbuf, negmu, rstd], outs=[ar])
        ev_p2_xn[i] = inc("v", s_v)

    ssm_slots = [(ps_ssm[0], 0, None), (ps_ssm[1], 0, None),
                 (ps_trF, 0, "trF:h0"), (ps_trF, 512, "trF:h1")]
    ssm_slots_t0 = [(ps_ssm[0], 0, None), (ps_ssm[1], 0, None),
                    (ps_out, 0, "out:h0"), (ps_out, 512, "out:h1"),
                    (ps_trF, 0, "trF:h0"), (ps_trF, 512, "trF:h1"),
                    (ps_ssm[0], 0, None), (ps_ssm[1], 0, None)]

    def grp_ssm_ffn1(i, c, spl):
        bank, off, skey = ssm_slots_t0[c] if i == 0 else ssm_slots[c % 4]
        for sem, val in spl[0]:
            wait("t", sem, val)
        if skey:
            for sem, val in slot_free.get(skey, []):
                wait("t", sem, val)
        else:
            slot_wait("t", bank)
        Xh, Xl = xfh[:, c, :], xfl[:, c, :]
        bk = bank[:, off:off + 387]
        mm(bk, Xh, gh[:, c, :], True, False)
        mm(bk, Xh, gl[:, c, :], False, False)
        for sem, val in spl[1]:
            wait("t", sem, val)
        ev_ssm = mm(bk, Xl, gh[:, c, :], False, True, inc=s_t)
        if c == 0:
            if st["psh_rd0"]:
                wait("t", s_v, st["psh_rd0"])
            mm(ps_h[:, H0], Xh, fwh[0][:, c, H0], True, False)
            mm(ps_h[:, H0], Xh, fwl[0][:, c, H0], False, False)
            mm(ps_h[:, H0], Xl, fwh[0][:, c, H0], False, False)
            if st["psh_rd"]:
                wait("t", s_v, st["psh_rd"])
            mm(ps_h[:, H1], Xh, fwh[0][:, c, H1], True, False)
            mm(ps_h[:, H1], Xh, fwl[0][:, c, H1], False, False)
            ev_grp = mm(ps_h[:, H1], Xl, fwh[0][:, c, H1], False, False, inc=s_t)
            return ev_ssm, None, ev_grp
        if c == NBLK - 1:
            mm(ps_h[:, H0], Xh, fwh[0][:, c, H0], False, False)
            mm(ps_h[:, H0], Xh, fwl[0][:, c, H0], False, False)
            ev_h0 = mm(ps_h[:, H0], Xl, fwh[0][:, c, H0], False, True, inc=s_t)
            mm(ps_h[:, H1], Xh, fwh[0][:, c, H1], False, False)
            mm(ps_h[:, H1], Xh, fwl[0][:, c, H1], False, False)
            ev_grp = mm(ps_h[:, H1], Xl, fwh[0][:, c, H1], False, True, inc=s_t)
            return ev_ssm, ev_h0, ev_grp
        mm(ps_h[:, H0], Xh, fwh[0][:, c, H0], c == 0, False)
        mm(ps_h[:, H1], Xh, fwh[0][:, c, H1], c == 0, False)
        mm(ps_h[:, H0], Xh, fwl[0][:, c, H0], False, False)
        mm(ps_h[:, H1], Xh, fwl[0][:, c, H1], False, False)
        mm(ps_h[:, H0], Xl, fwh[0][:, c, H0], False, False)
        ev_grp = mm(ps_h[:, H1], Xl, fwh[0][:, c, H1], False, False, inc=s_t)
        return ev_ssm, None, ev_grp

    def blend(i, c, ev_ssm):
        bank, off, skey = ssm_slots_t0[c] if i == 0 else ssm_slots[c % 4]
        ar = arena[i % 2]
        wait("v", s_t, ev_ssm)
        if i == 0 and c == 0:
            wait("v", s_c, ev_bc)
        op("v", lambda E, bank=bank, off=off: E.tensor_scalar(
            g3[:, :], bank[:, off + 384:off + 387], c01b[:, 0:1], omcb[:, 0:1],
            ALU.mult, ALU.add), ins=[c01b, omcb], outs=[g3])
        for t3 in range(3):
            op("v", lambda E, bank=bank, off=off, t3=t3, c=c, ar=ar:
               E.scalar_tensor_tensor(ar[:, c * 128:(c + 1) * 128],
                                      bank[:, off + t3 * 128:off + (t3 + 1) * 128],
                                      g3[:, t3:t3 + 1], ar[:, c * 128:(c + 1) * 128],
                                      ALU.mult, ALU.add),
               ins=[g3, ar], outs=[ar])
        ev = inc("v", s_v)
        slot_free[_nm(bank[:, 0:1])] = [(s_v, ev)]
        if skey:
            slot_free[skey] = [(s_v, ev)]
        return ev

    def wide_grp(stat_h, stat_l, c, wh, wl, pdst, start, stop, hi_wait=(), lo_wait=(),
                 inc_last=None, h0_first=False):
        for sem, val in hi_wait:
            wait("t", sem, val)
        Sh, Sl = stat_h[:, c, :], stat_l[:, c, :]
        if h0_first:
            mm(pdst[:, H0], Sh, wh[:, c, H0], start, False)
            mm(pdst[:, H0], Sh, wl[:, c, H0], False, False)
            for sem, val in lo_wait:
                wait("t", sem, val)
            ev_h0 = mm(pdst[:, H0], Sl, wh[:, c, H0], False, stop, inc=inc_last)
            mm(pdst[:, H1], Sh, wh[:, c, H1], start, False)
            mm(pdst[:, H1], Sh, wl[:, c, H1], False, False)
            return ev_h0, mm(pdst[:, H1], Sl, wh[:, c, H1], False, stop, inc=inc_last)
        mm(pdst[:, H0], Sh, wh[:, c, H0], start, False)
        mm(pdst[:, H1], Sh, wh[:, c, H1], start, False)
        mm(pdst[:, H0], Sh, wl[:, c, H0], False, False)
        mm(pdst[:, H1], Sh, wl[:, c, H1], False, False)
        for sem, val in lo_wait:
            wait("t", sem, val)
        ev_h0 = mm(pdst[:, H0], Sl, wh[:, c, H0], False, stop, inc=inc_last)
        return ev_h0, mm(pdst[:, H1], Sl, wh[:, c, H1], False, stop, inc=inc_last)


    wait("v", s_t, ev_p1_gram[n_tiles - 1])
    op("v", lambda E: E.tensor_tensor(outbuf[:, :], ps_out[:, :], u0[:, :], ALU.mult),
       ins=[u0, outbuf], outs=[outbuf])
    op("v", lambda E: E.reduce_sum(negmu[:, 0:1], outbuf[:, :], axis=AX.X),
       ins=[outbuf], outs=[negmu])
    ev_red = inc("v", s_v)
    slot_free["out:h0"] = [(s_v, ev_red)]
    slot_free["out:h1"] = [(s_v, ev_red)]
    wait("t", s_v, ev_red)
    op("t", lambda E: E.transpose(ps_trF[0:1, 0:128], negmu[:, 0:1], eye32[:, :]))
    ev_tt = inc("t", s_t)
    wait("v", s_t, ev_tt)
    op("v", lambda E: E.reduce_sum(sca[0:1, 1:2], ps_trF[0:1, 0:128], axis=AX.X),
       ins=[sca], outs=[sca])
    op("v", lambda E: E.tensor_scalar(sca[0:1, 2:3], sca[0:1, 1:2], 1.0 / float(S_loc),
                                      None, ALU.mult), ins=[sca], outs=[sca])
    ev_v1 = inc("v", s_v)
    wait("s", s_v, ev_v1)
    op("s", lambda E: E.activation(sca[0:1, 3:4], sca[0:1, 0:1], ACT.Ln), ins=[sca], outs=[sca])
    ev_s1 = inc("s", s_s)
    wait("v", s_s, ev_s1)
    op("v", lambda E: E.reciprocal(sca[0:1, 4:5], sca[0:1, 3:4]), ins=[sca], outs=[sca])
    op("v", lambda E: E.tensor_scalar(sca[0:1, 5:6], sca[0:1, 2:3], 10.0, None, ALU.mult),
       ins=[sca], outs=[sca])
    op("v", lambda E: E.tensor_tensor(sca[0:1, 6:7], sca[0:1, 5:6], sca[0:1, 4:5], ALU.mult),
       ins=[sca], outs=[sca])
    op("v", lambda E: E.tensor_scalar(sca[0:1, 7:8], sca[0:1, 6:7], -1.0, CLIP,
                                      ALU.mult, ALU.min), ins=[sca], outs=[sca])
    op("v", lambda E: E.tensor_scalar(sca[0:1, 7:8], sca[0:1, 7:8], -CLIP, None, ALU.max),
       ins=[sca], outs=[sca])
    ev_v2 = inc("v", s_v)
    wait("s", s_v, ev_v2)
    op("s", lambda E: E.activation(sca[0:1, 8:9], sca[0:1, 7:8], ACT.Exp), ins=[sca], outs=[sca])
    ev_s2 = inc("s", s_s)
    wait("v", s_s, ev_s2)
    op("v", lambda E: E.tensor_scalar(sca[0:1, 9:10], sca[0:1, 8:9], 1.0, None, ALU.add),
       ins=[sca], outs=[sca])
    op("v", lambda E: E.reciprocal(sca[0:1, 10:11], sca[0:1, 9:10]), ins=[sca], outs=[sca])
    op("v", lambda E: E.tensor_scalar(ccv[:, :], oh_sb[:, :], sca[0:1, 10:11], None, ALU.mult),
       ins=[oh_sb, sca], outs=[ccv])
    ev_ccv = inc("v", s_v)
    wait("g", s_v, ev_ccv)
    dma("g", cc_in[:, :].opt(), ccv[:, :], s_c, outs=[])
    b.wait("g", s_c, b.cnt[s_c])
    op("g", lambda E: E.collective_compute("AllReduce", ALU.add,
                                           replica_groups=[list(range(B))],
                                           ins=[cc_in.ap().opt()], outs=[cc_out.ap().opt()]),
       inc=s_cc, amt=1)
    b.wait("g", s_cc, b.cnt[s_cc])
    dma("g", alldwc[:, :], cc_out[:, :].opt(), s_c)
    ev_adwc = b.cnt[s_c]
    p2_load(0)
    p2_load(1)
    p2_preamble(0)
    halves_nxt = tr_phase(arena[0][:, :], xnh_fm[:, :], xnl_fm[:, :],
                          src_wait=[(s_v, ev_p2_xn[0])], hi_eng="v")[0]

    wait("v", s_c, ev_adwc)
    op("v", lambda E: E.reduce_sum(sca[0:1, 11:12], alldwc[:, :], axis=AX.X),
       ins=[alldwc, sca], outs=[sca])
    op("v", lambda E: E.tensor_scalar(sca[0:1, 12:13], sca[0:1, 11:12], 0.125, None, ALU.mult),
       ins=[sca], outs=[sca])
    op("v", lambda E: E.tensor_scalar(sca[0:1, 13:14], sca[0:1, 12:13], 0.99, None, ALU.is_gt),
       ins=[sca], outs=[sca])
    op("v", lambda E: E.tensor_scalar(sca[0:1, 14:15], sca[0:1, 12:13], 0.01, None, ALU.is_lt),
       ins=[sca], outs=[sca])
    op("v", lambda E: E.tensor_scalar(sca[0:1, 15:16], sca[0:1, 13:14], -1.0, 1.0,
                                      ALU.mult, ALU.add), ins=[sca], outs=[sca])
    op("v", lambda E: E.tensor_scalar(sca[0:1, 16:17], sca[0:1, 14:15], -1.0, 1.0,
                                      ALU.mult, ALU.add), ins=[sca], outs=[sca])
    op("v", lambda E: E.tensor_tensor(sca[0:1, 17:18], sca[0:1, 15:16], sca[0:1, 16:17],
                                      ALU.mult), ins=[sca], outs=[sca])
    op("v", lambda E: E.tensor_tensor(sca[0:1, 18:19], sca[0:1, 17:18], sca[0:1, 10:11],
                                      ALU.mult), ins=[sca], outs=[sca])
    op("v", lambda E: E.tensor_tensor(sca[0:1, 19:20], sca[0:1, 18:19], sca[0:1, 13:14],
                                      ALU.add), ins=[sca], outs=[sca])
    op("v", lambda E: E.tensor_scalar(sca[0:1, 20:21], sca[0:1, 19:20], 0.1, None, ALU.mult),
       ins=[sca], outs=[sca])
    op("v", lambda E: E.tensor_scalar(sca[0:1, 21:22], sca[0:1, 19:20], -1.0, 1.0,
                                      ALU.mult, ALU.add), ins=[sca], outs=[sca])
    ev_coef = inc("v", s_v)
    st["out_rd"] = ev_coef
    wait("g", s_v, ev_coef)
    dma("g", scr2[:, :].opt(), sca[0:1, 20:22], s_c, outs=[])
    b.wait("g", s_c, b.cnt[s_c])
    dma("g", com2[:, :], scr2.ap()[0:1, 0:2].partition_broadcast(128), s_c)
    ev_bc = b.cnt[s_c]

    def tail_post(p):
        i0, ar0 = p["i"], p["ar"]
        wait("v", s_s, p["ev_exp"])
        wait("v", s_t, p["ev_wo_t"])
        if i0 == 0:
            wait("v", s_c, TK["bao"])
        op("v", lambda E, i0=i0: E.scalar_tensor_tensor(outbuf[:, :], ps_out[:, :], 1.0,
                                                        xbuf[:, i0 % 2, :], ALU.mult, ALU.add),
           ins=[xbuf, outbuf], outs=[outbuf])
        op("v", lambda E, ar0=ar0: E.tensor_tensor(outbuf[:, :], outbuf[:, :], ar0[:, :],
                                                   ALU.add), ins=[outbuf, ar0], outs=[outbuf])
        op("v", lambda E: E.tensor_tensor(outbuf[:, :], outbuf[:, :], bao_rep[:, :],
                                          ALU.add), ins=[outbuf, bao_rep], outs=[outbuf])
        ev_fin = inc("v", s_v)
        st["out_rd"] = ev_fin
        st["xfree"][i0] = ev_fin
        wait("g", s_v, ev_fin)
        dma("g", out_hbm[i0 * 128:(i0 + 1) * 128, :], outbuf[:, :], s_o, outs=[])
        st["store"][i0] = b.cnt[s_o]
        p2_load(i0 + 2)

    prev = None
    for i in range(n_tiles):
        ar = arena[i % 2]
        halves = halves_nxt
        for c in range(NBLK):
            ev_ssm, ev_f1h0, ev_grp = grp_ssm_ffn1(i, c, halves[c // 4])
            blend(i, c, ev_ssm)
        ev_ffn1 = b.cnt[s_t]

        wait("s", s_t, ev_f1h0)
        op("s", lambda E: E.copy(u0[:, 0:512], ps_h[:, 0:512]), ins=[u0], outs=[u0])
        ev_h32a = inc("s", s_s)
        wait("s", s_t, ev_ffn1)
        op("s", lambda E: E.copy(u0[:, 512:1024], ps_h[:, 512:1024]), ins=[u0], outs=[u0])
        ev_h32 = inc("s", s_s)
        h1_halves, h1_q = tr_phase(u0[:, :], h1fm_h, h1fm_l, src_wait=[(s_s, ev_h32a)],
                                   src_wait2=[(s_s, ev_h32)])

        for c in range(NBLK - 1):
            wide_grp(h1h, h1l, c, fwh[1], fwl[1], ps_h, c == 0, False,
                     hi_wait=[h1_q[c]], lo_wait=h1_halves[c // 4][1])

        if prev is not None:
            tail_post(prev)

        ln_stats(ar)
        op("v", lambda E, ar=ar: E.tensor_scalar(ar[:, :], ar[:, :], negmu[:, 0:1],
                                                 rstd[:, 0:1], ALU.add, ALU.mult),
           ins=[ar, negmu, rstd], outs=[ar])
        ev_norm = inc("v", s_v)

        n_halves, n_q = tr_phase(ar[:, :], nfm_h, nfm_l, src_wait=[(s_v, ev_norm)])
        ev_ffn2 = wide_grp(h1h, h1l, NBLK - 1, fwh[1], fwl[1], ps_h, False, True,
                           hi_wait=h1_halves[1][0], lo_wait=h1_halves[1][1],
                           inc_last=s_t, h0_first=True)
        wait("s", s_t, ev_ffn2[0])
        op("s", lambda E: E.copy(u0[:, 0:512], ps_h[:, 0:512]), ins=[u0], outs=[u0])
        ev_h32c = inc("s", s_s)
        wait("s", s_t, ev_ffn2[1])
        op("s", lambda E: E.copy(u0[:, 512:1024], ps_h[:, 512:1024]), ins=[u0], outs=[u0])
        ev_h32b = inc("s", s_s)

        p2_preamble(i + 1)

        for c in range(NBLK - 1):
            ew = [n_q[c]]
            if c == 0:
                ew.append((s_v, st["out_rd"]))
                ew += slot_free.get(_nm(ps_out[:, 0:1]), [])
            wide_grp(nfh, nfl, c, woh, wol, ps_out, c == 0, False,
                     hi_wait=ew, lo_wait=n_halves[c // 4][1])

        h2_halves, h2_q = tr_phase(u0[:, :], h2fm_h, h2fm_l, src_wait=[(s_s, ev_h32c)],
                                   src_wait2=[(s_s, ev_h32b)])
        ev_wo_t = wide_grp(nfh, nfl, NBLK - 1, woh, wol, ps_out, False, True,
                           hi_wait=n_halves[1][0], lo_wait=n_halves[1][1],
                           inc_last=s_t)[1]

        for c in range(2):
            wide_grp(h2h, h2l, c, fwh[2], fwl[2], ps_h, c == 0, False,
                     hi_wait=[h2_q[c]], lo_wait=h2_halves[c // 4][1])
        if i + 1 < n_tiles:
            halves_nxt = tr_phase(arena[(i + 1) % 2][:, :], xnh_fm[:, :], xnl_fm[:, :],
                                  src_wait=[(s_v, ev_p2_xn[i + 1])], hi_eng="v")[0]
        for c in range(2, NBLK):
            ev_ffn3 = wide_grp(h2h, h2l, c, fwh[2], fwl[2], ps_h, False, c == NBLK - 1,
                               hi_wait=[h2_q[c]], lo_wait=h2_halves[c // 4][1],
                               inc_last=s_t if c == NBLK - 1 else None,
                               h0_first=(c == NBLK - 1))

        op("s", lambda E: E.activation(sca[0:1, 30:31], sca[0:1, 30:31], ACT.Tanh),
           ins=[sca], outs=[sca])
        wait("v", s_t, ev_ffn3[0])
        if i == 0:
            wait("v", s_w, ev_w0)
        wait("v", s_o, st["store"][i - 1] if i >= 1 else 0)
        op("v", lambda E: E.tensor_tensor(outbuf[:, H0], ps_h[:, H0], fb_rep[:, H0],
                                          ALU.add), ins=[fb_rep, outbuf], outs=[outbuf])
        ev_fb0 = inc("v", s_v)
        st["psh_rd0"] = ev_fb0
        wait("v", s_t, ev_ffn3[1])
        op("v", lambda E: E.tensor_tensor(outbuf[:, H1], ps_h[:, H1], fb_rep[:, H1],
                                          ALU.add), ins=[fb_rep, outbuf], outs=[outbuf])
        ev_fb = inc("v", s_v)
        st["psh_rd"] = ev_fb
        wait("s", s_v, ev_fb)
        op("s", lambda E, ar=ar: E.activation(ar[:, :], outbuf[:, :], ACT.Tanh),
           ins=[outbuf], outs=[ar])
        ev_exp = inc("s", s_s)
        op("s", lambda E: E.activation(sca[0:1, 31:32], epst[0:1, 0:1], ACT.Ln,
                                       bias=epst[0:1, 0:1]), ins=[sca, epst], outs=[sca])
        prev = {"i": i, "ar": ar, "ev_wo_t": ev_wo_t, "ev_exp": ev_exp}

    tail_post(prev)
    b.wait("g", s_o, b.cnt[s_o])

    block = es.enter_context(nc.Block())
    with es:
        @block.sync
        def _(E):
            for f in b.SCH["y"]:
                f(E)

        @block.vector
        def _(E):
            for f in b.SCH["v"]:
                f(E)

        @block.scalar
        def _(E):
            for f in b.SCH["s"]:
                f(E)

        @block.tensor
        def _(E):
            for f in b.SCH["t"]:
                f(E)

        @block.gpsimd
        def _(E):
            for f in b.SCH["g"]:
                f(E)
    return nc


_CACHE = {}


def _get_nc(n_tiles):
    if n_tiles not in _CACHE:
        _CACHE[n_tiles] = build_nc(n_tiles)
    return _CACHE[n_tiles]


def kernel(L0_attn_x, L0_attn_W_q, L0_attn_W_k, L0_attn_W_v, L0_attn_W_o,
           L0_attn_SSM_A, L0_attn_SSM_B, L0_attn_SSM_C, L0_attn_ln_gamma,
           L0_attn_ln_beta, L0_ffn_weight, L0_ffn_aux_weight1, L0_ffn_aux_weight2,
           L0_ffn_bias):
    x = np.ascontiguousarray(np.asarray(L0_attn_x, dtype=np.float32))
    Bsz, S_, D_ = x.shape
    assert D_ == D and Bsz == B
    n_tiles = S_ // 128
    idx = np.arange(NBLK)

    def diag(M):
        M = np.asarray(M, dtype=np.float32)
        return np.ascontiguousarray(M.reshape(NBLK, 128, NBLK, 128)[idx, :, idx, :])

    common = {
        "wq": np.ascontiguousarray(np.asarray(L0_attn_W_q, dtype=np.float32)),
        "wk": np.ascontiguousarray(np.asarray(L0_attn_W_k, dtype=np.float32)),
        "wv": np.ascontiguousarray(np.asarray(L0_attn_W_v, dtype=np.float32)),
        "ad": diag(L0_attn_SSM_A),
        "bd": diag(L0_attn_SSM_B),
        "cd": diag(L0_attn_SSM_C),
        "wo": np.ascontiguousarray(np.asarray(L0_attn_W_o, dtype=np.float32)),
        "gamma": np.ascontiguousarray(np.asarray(L0_attn_ln_gamma, dtype=np.float32)),
        "beta": np.ascontiguousarray(np.asarray(L0_attn_ln_beta, dtype=np.float32)),
        "f1": np.ascontiguousarray(np.asarray(L0_ffn_weight, dtype=np.float32)),
        "f2": np.ascontiguousarray(np.asarray(L0_ffn_aux_weight1, dtype=np.float32)),
        "f3": np.ascontiguousarray(np.asarray(L0_ffn_aux_weight2, dtype=np.float32)),
        "fb": np.ascontiguousarray(np.asarray(L0_ffn_bias, dtype=np.float32)),
        "eye": np.eye(128, dtype=np.float32),
    }
    eyeB = np.eye(B, dtype=np.float32)
    in_maps = [dict(common, x=np.ascontiguousarray(x[bb]),
                    onehot=np.ascontiguousarray(eyeB[bb:bb + 1]))
               for bb in range(B)]
    nc = _get_nc(n_tiles)
    res = run_bass_kernel_spmd(nc, in_maps, core_ids=list(range(B)))
    return np.stack([np.asarray(r["out"]) for r in res.results], axis=0)


# version 38
# speedup vs baseline: 1.0174x; 1.0037x over previous
import contextlib
import numpy as np

import concourse.bass as bass
import concourse.mybir as mybir
from concourse.bass_utils import run_bass_kernel_spmd

F32 = mybir.dt.float32
BF16 = mybir.dt.bfloat16
AX = mybir.AxisListType
ALU = mybir.AluOpType
ACT = mybir.ActivationFunctionType

B = 8
D = 1024
NBLK = 8
EPS = 1e-5
CLIP = 88.0
H0 = slice(0, 512)
H1 = slice(512, 1024)


def _nm(ap):
    n = ap.name
    return n() if callable(n) else n


class Bld:

    def __init__(self):
        self.SCH = {k: [] for k in ("t", "v", "s", "y", "g")}
        self.cnt = {}
        self.dirty = {k: set() for k in self.SCH}
        self.wm = {}

    def op(self, k, f, ins=(), outs=(), inc=None, amt=1):
        nm = {_nm(a) for a in ins} | {_nm(a) for a in outs}
        if k != "t" and (nm & self.dirty[k]):
            self.SCH[k].append(lambda E: E.drain())
            self.dirty[k].clear()
        if inc is not None:
            self.cnt[inc] = self.cnt.get(inc, 0) + amt
            self.SCH[k].append(lambda E, f=f, inc=inc, amt=amt: f(E).then_inc(inc, amt))
            ret = self.cnt[inc]
        else:
            self.SCH[k].append(lambda E, f=f: f(E))
            ret = None
        self.dirty[k] |= {_nm(a) for a in outs}
        return ret

    def wait(self, k, sem, val):
        if not val:
            return
        assert val <= self.cnt.get(sem, 0), f"wait beyond scheduled ({val} > {self.cnt.get(sem, 0)})"
        wm = self.wm.setdefault(k, {})
        if wm.get(sem, -1) >= val:
            return
        wm[sem] = val
        self.SCH[k].append(lambda E, sem=sem, val=val: E.wait_ge(sem, val))

    def inc(self, k, sem, amt=1):
        self.cnt[sem] = self.cnt.get(sem, 0) + amt
        self.SCH[k].append(lambda E, sem=sem, amt=amt: E.drain().then_inc(sem, amt))
        self.dirty[k].clear()
        return self.cnt[sem]


def build_nc(n_tiles):
    nc = bass.Bass("TRN2", target_bir_lowering=False, debug=False)
    S_loc = n_tiles * 128
    assert n_tiles % 2 == 0 and n_tiles >= 4

    x_hbm = nc.dram_tensor("x", [S_loc, D], F32, kind="ExternalInput").ap()
    w_hbm = {}
    for name, shp in (("wq", [NBLK, 128, 128]), ("wk", [NBLK, 128, 128]),
                      ("wv", [NBLK, 128, 128]), ("ad", [NBLK, 128, 128]),
                      ("bd", [NBLK, 128, 128]), ("cd", [NBLK, 128, 128]),
                      ("wo", [D, D]), ("gamma", [D]), ("beta", [D]),
                      ("f1", [D, D]), ("f2", [D, D]), ("f3", [D, D]), ("fb", [D]),
                      ("eye", [128, 128]), ("onehot", [1, B])):
        w_hbm[name] = nc.dram_tensor(name, shp, F32, kind="ExternalInput").ap()
    out_hbm = nc.dram_tensor("out", [S_loc, D], F32, kind="ExternalOutput").ap()
    cc_in = nc.dram_tensor("cc_in", [1, B], F32)
    cc_out = nc.dram_tensor("cc_out", [1, B], F32)
    scr2 = nc.dram_tensor("scr2", [1, 2], F32)
    scrb = nc.dram_tensor("scrb", [1, D], F32)

    es = contextlib.ExitStack()
    sb = lambda n, s, d=F32: es.enter_context(nc.sbuf_tensor(n, s, d))

    gh = sb("gh", [128, NBLK, 387], BF16)
    gl = sb("gl", [128, NBLK, 387], BF16)
    woh = sb("woh", [128, NBLK, D], BF16)
    wol = sb("wol", [128, NBLK, D], BF16)
    fwh = [sb(f"f{j}h", [128, NBLK, D], BF16) for j in (1, 2, 3)]
    fwl = [sb(f"f{j}l", [128, NBLK, D], BF16) for j in (1, 2, 3)]
    bao_rep = sb("bao_rep", [128, D], F32)
    fb_rep = sb("fb_rep", [128, D], F32)
    bao_row = sb("bao_row", [1, D], F32)
    ones_row = sb("ones_row", [1, 128], BF16)
    eye32 = sb("eye32", [128, 128], F32)
    gm = sb("gm", [128, NBLK], F32)
    bm32 = sb("bm32", [128, NBLK], F32)
    bmh = sb("bmh", [128, NBLK], BF16)
    bml = sb("bml", [128, NBLK], BF16)

    xbuf = sb("xbuf", [128, 2, D], F32)
    xn = sb("xn", [128, D], F32)
    wstage = sb("wstage", [128, D], F32)
    fstage = sb("fstage", [128, D], F32)
    xnh_fm = sb("xnh_fm", [128, D], BF16)
    xnl_fm = sb("xnl_fm", [128, D], BF16)
    u0 = sb("u0", [128, D], F32)
    u4k_0 = sb("u4k_0", [128, 2048], BF16)
    u4k_1 = sb("u4k_1", [128, 2048], BF16)
    u4k_2 = sb("u4k_2", [128, 2048], BF16)
    outbuf = sb("outbuf", [128, D], F32)
    stats6 = sb("stats6", [128, 2, 2, 6], F32)
    mv = sb("mv", [128, 2, 2], F32)
    negmu = sb("negmu", [128, 2], F32)
    la = sb("la", [128, 2], F32)
    rstd = sb("rstd", [128, 2], F32)
    nmr = sb("nmr", [128, 2], F32)
    epst = sb("epst", [128, 1], F32)
    nhc = sb("nhc", [128, 1], F32)
    g3 = sb("g3", [128, 3], F32)
    com2 = sb("com2", [128, 2], F32)
    c01b = com2[:, 0:1]
    omcb = com2[:, 1:2]
    sca = sb("sca", [1, 32], F32)
    oh_sb = sb("oh_sb", [1, B], F32)
    ccv = sb("ccv", [1, B], F32)
    alldwc = sb("alldwc", [1, B], F32)

    stg32b = sb("stg32b", [128, 771], F32)
    stg32b2 = sb("stg32b2", [128, D], F32)
    stg32_p = [wstage[:, 0:771], stg32b[:, :]]
    stgh_p = [u4k_1[:, 0:771], u4k_2[:, 0:771]]
    stgl_p = [u4k_1[:, 1024:1795], u4k_2[:, 1024:1795]]
    wraw_h_p = [u4k_2[:, 0:1024], u4k_1[:, 0:1024]]
    wraw_l_p = [u4k_2[:, 1024:2048], u4k_1[:, 1024:2048]]
    p1set = None
    nfm_h = u4k_1[:, 0:1024]
    nfm_l = u4k_1[:, 1024:2048]
    h1fm_h = u4k_2[:, 0:1024]
    h1fm_l = u4k_2[:, 1024:2048]
    h2fm_h = u4k_0[:, 0:1024]
    h2fm_l = u4k_0[:, 1024:2048]
    m_view = u0[:, :].rearrange("p (n d) -> p n d", n=NBLK)
    p1set = [(u4k_0[:, 0:1024], u4k_0[:, 1024:2048]),
             (xnh_fm[:, :], xnl_fm[:, :])]

    ps_trF = es.enter_context(nc.psum_tensor("ps_trF", [128, 1024], F32))
    ps_ssm = [es.enter_context(nc.psum_tensor(f"ps_ssm{i}", [128, 512], F32)) for i in range(2)]
    ps_out = es.enter_context(nc.psum_tensor("ps_out", [128, 1024], F32))
    ps_h = es.enter_context(nc.psum_tensor("ps_h", [128, 1024], F32))

    s_x = es.enter_context(nc.semaphore("s_x"))
    s_w = es.enter_context(nc.semaphore("s_w"))
    s_o = es.enter_context(nc.semaphore("s_o"))
    s_c = es.enter_context(nc.semaphore("s_c"))
    s_cc = es.enter_context(nc.semaphore("s_cc"))
    s_t = es.enter_context(nc.semaphore("s_t"))
    s_v = es.enter_context(nc.semaphore("s_v"))
    s_s = es.enter_context(nc.semaphore("s_s"))
    s_g = es.enter_context(nc.semaphore("s_g"))

    b = Bld()
    op, wait, inc = b.op, b.wait, b.inc

    def dma(k, dst, src, sem, outs=None, slow=False, serialize=True):
        if serialize:
            wait(k, sem, b.cnt.get(sem, 0))
        return op(k, lambda E, dst=dst, src=src, slow=slow:
                  E.dma_start(out=dst, in_=src, allow_slow_non_contiguous=slow),
                  outs=outs if outs is not None else [dst], inc=sem, amt=16)

    def mm(pdst, lhsT, rhs, start, stop, inc=None):
        return op("t", lambda E, p=pdst, a=lhsT, bb=rhs, s1=start, s2=stop:
                  E.matmul(p, a, bb, start=s1, stop=s2), inc=inc)

    def mm3(pdst, ah, al, bh, bl, start, stop, inc=None):
        mm(pdst, ah, bh, start, False)
        mm(pdst, ah, bl, False, False)
        return mm(pdst, al, bh, False, stop, inc=inc)

    def split_s(hi_dst, src, ins=()):
        op("s", lambda E, hi_dst=hi_dst, src=src: E.copy(hi_dst, src),
           ins=list(ins) + [src], outs=[hi_dst])

    def split_v(lo_dst, src, hi, ins=()):
        op("v", lambda E, lo_dst=lo_dst, src=src, hi=hi:
           E.scalar_tensor_tensor(lo_dst, src, 1.0, hi, ALU.mult, ALU.subtract),
           ins=list(ins) + [src, hi], outs=[lo_dst])

    def ln_stats(src_ap, p=0):
        for c2 in range(2):
            op("v", lambda E, c2=c2, src_ap=src_ap, p=p:
               E.bn_stats(stats6[:, p, c2, :], src_ap[:, c2 * 512:(c2 + 1) * 512]),
               ins=[src_ap], outs=[stats6])
        op("v", lambda E, p=p: E.bn_aggr(mv[:, p, :],
                                         stats6[:, p, :, :].rearrange("p c s -> p (c s)")),
           ins=[stats6], outs=[mv])
        op("v", lambda E, p=p: E.tensor_scalar(negmu[:, p:p + 1], mv[:, p, 0:1], -1.0,
                                               None, ALU.mult), ins=[mv], outs=[negmu])
        ev = inc("v", s_v)
        wait("s", s_v, ev)
        op("s", lambda E, p=p: E.activation(la[:, p:p + 1], mv[:, p, 1:2], ACT.Ln,
                                            bias=epst[:, 0:1]), ins=[mv, epst], outs=[la])
        op("s", lambda E, p=p: E.activation(rstd[:, p:p + 1], la[:, p:p + 1], ACT.Exp,
                                            scale=nhc[:, 0:1]), ins=[la, nhc], outs=[rstd])
        ev_s = inc("s", s_s)
        wait("v", s_s, ev_s)

    slot_free = {}

    def slot_wait(k, p):
        for sem, val in slot_free.get(_nm(p[:, 0:1]), []):
            wait(k, sem, val)

    def tr_phase(src32_ap, hi_dst, lo_dst, src_wait=(), src_wait2=(), hi_eng="s",
                 src_wait_mid=()):
        slot_wait("t", ps_trF)
        for sem, val in src_wait:
            wait("t", sem, val)
        halves = []
        for half in range(2):
            if half == 1:
                for sem, val in src_wait2:
                    wait("t", sem, val)
            ev_t = None
            for c in range(4 * half, 4 * half + 4):
                if half == 0 and c == 2:
                    for sem, val in src_wait_mid:
                        wait("t", sem, val)
                ev_t = op("t", lambda E, c=c, src32_ap=src32_ap:
                          E.transpose(ps_trF[:, c * 128:(c + 1) * 128],
                                      src32_ap[:, c * 128:(c + 1) * 128], eye32[:, :]),
                          inc=s_t if c % 4 == 3 else None)
            hs = slice(half * 512, (half + 1) * 512)
            if hi_eng == "s":
                wait("s", s_t, ev_t)
                split_s(hi_dst[:, hs], ps_trF[:, hs])
                se = inc("s", s_s)
                hi_ev = (s_s, se)
            else:
                wait("v", s_t, ev_t)
                op("v", lambda E, hs=hs, hi_dst=hi_dst:
                   E.tensor_copy(hi_dst[:, hs], ps_trF[:, hs]), outs=[hi_dst])
                hi_ev = (s_v, inc("v", s_v))
            wait("v", s_t, ev_t)
            wait("v", hi_ev[0], hi_ev[1])
            split_v(lo_dst[:, hs], ps_trF[:, hs], hi_dst[:, hs])
            ve = inc("v", s_v)
            halves.append(([hi_ev], [(s_v, ve)]))
        slot_free["trF:h0"] = halves[0][0] + halves[0][1]
        slot_free["trF:h1"] = halves[1][0] + halves[1][1]
        slot_free[_nm(ps_trF[:, 0:1])] = (halves[0][0] + halves[0][1]
                                          + halves[1][0] + halves[1][1])
        return halves

    dma("y", eye32[:, :], w_hbm["eye"][:, :], s_w)
    dma("y", gm[:, :], w_hbm["gamma"].rearrange("(c d) -> d c", d=128), s_w, slow=True)
    dma("y", bm32[:, :], w_hbm["beta"].rearrange("(c d) -> d c", d=128), s_w, slow=True)
    dma("y", fb_rep[:, :], w_hbm["fb"].unsqueeze(0).partition_broadcast(128), s_w)
    dma("y", oh_sb[:, :], w_hbm["onehot"][:, :], s_w)
    ev_w0 = b.cnt[s_w]

    for f in (lambda E: E.memset(epst[:, :], EPS),
              lambda E: E.memset(nhc[:, :], -0.5),
              lambda E: E.memset(ones_row[:, :], 1.0),
              lambda E: E.memset(sca[:, :], 0.0)):
        op("v", f, outs=[epst, nhc, ones_row, sca])
    for stg in stg32_p:
        for col in (256, 513, 770):
            op("v", lambda E, stg=stg, col=col: E.memset(stg[:, col:col + 1], 1.0),
               ins=[stg], outs=[stg])
    ev_ones0 = inc("v", s_v)
    op("v", lambda E: E.memset(sca[0:1, 0:1], float(S_loc) + 1.0), ins=[sca], outs=[sca])

    ev_s0 = inc("s", s_s)
    wait("v", s_w, ev_w0)
    ev_v0 = inc("v", s_v)

    arena = [xn, wstage]
    p1arena = [xn, outbuf]

    ev_p1_split = [None] * n_tiles
    ev_p1_xrd = [None] * n_tiles
    ev_p1_gram = [None] * n_tiles

    def p1_load(i):
        if i >= 2:
            wait("y", s_s, ev_p1_xrd[i - 2])
        dma("y", xbuf[:, i % 2, :], x_hbm[i * 128:(i + 1) * 128, :], s_x, outs=[xbuf])

    def p1_preamble(i):
        p = i % 2
        wait("v", s_x, 16 * (i + 1))
        for c2 in range(2):
            op("v", lambda E, c2=c2, i=i, p=p:
               E.bn_stats(stats6[:, p, c2, :], xbuf[:, i % 2, c2 * 512:(c2 + 1) * 512]),
               ins=[xbuf], outs=[stats6])
        op("v", lambda E, p=p: E.bn_aggr(mv[:, p, :],
                                         stats6[:, p, :, :].rearrange("p c s -> p (c s)")),
           ins=[stats6], outs=[mv])
        op("v", lambda E, p=p: E.tensor_scalar(negmu[:, p:p + 1], mv[:, p, 0:1], -1.0,
                                               None, ALU.mult), ins=[mv], outs=[negmu])
        ev_v = inc("v", s_v)
        wait("s", s_v, ev_v)
        op("s", lambda E, p=p: E.activation(la[:, p:p + 1], mv[:, p, 1:2], ACT.Ln,
                                            bias=epst[:, 0:1]), ins=[mv, epst], outs=[la])
        op("s", lambda E, p=p: E.activation(rstd[:, p:p + 1], la[:, p:p + 1], ACT.Exp,
                                            scale=nhc[:, 0:1]), ins=[la, nhc], outs=[rstd])
        op("s", lambda E, p=p: E.activation(nmr[:, p:p + 1], negmu[:, p:p + 1],
                                            ACT.Identity, scale=rstd[:, p:p + 1]),
           ins=[negmu, rstd], outs=[nmr])
        ar = p1arena[i % 2]
        if i >= 2:
            wait("s", s_t, ev_p1_gram[i - 2])
        op("s", lambda E, i=i, ar=ar, p=p: E.activation(ar[:, :], xbuf[:, i % 2, :],
                                                        ACT.Identity,
                                                        bias=nmr[:, p:p + 1],
                                                        scale=rstd[:, p:p + 1]),
           ins=[xbuf, nmr, rstd], outs=[ar])
        ev_xn = inc("s", s_s)
        ev_p1_xrd[i] = ev_xn
        ev_p1_split[i] = ev_xn

    def p1_gram(i):
        ar = p1arena[i % 2]
        wait("t", s_s, ev_p1_split[i])
        for n in range(NBLK):
            csl = slice(n * 128, (n + 1) * 128)
            first = (i == 0) and (n in (0, 4))
            last = (i == n_tiles - 1) and (n in (3, 7))
            ev_p1_gram[i] = mm(ps_out[:, csl], ar[:, csl], ar[:, csl], first, last,
                               inc=s_t if n == NBLK - 1 else None)

    TK = {"split": (ev_s0, ev_v0), "evac_s": None, "evac_v": None, "gmm": None,
          "bias_t": None, "wo": None, "f": None, "bao": None, "gmm_final": None,
          "fp": {0: None, 1: None}}

    def g_task(n):
        stg32 = stg32_p[n % 2]
        wait("y", s_v, ev_ones0)
        if TK["gmm"] is not None:
            wait("y", s_t, TK["gmm"])
        wait("y", s_w, b.cnt.get(s_w, 0))
        for dst0, srcn in ((0, "wq"), (128, "ad"), (257, "wk"), (385, "bd"),
                           (514, "wv"), (642, "cd")):
            dma("y", stg32[:, dst0:dst0 + 128], w_hbm[srcn][n], s_w, outs=[stg32],
                serialize=False)
        ev_ld = b.cnt[s_w]
        return lambda: g_back(n, ev_ld)

    def g_back(n, ev_ld):
        stg32 = stg32_p[n % 2]
        wait("t", s_w, ev_ld)
        wait("t", s_v, ev_ones0)
        if TK["evac_s"] is not None:
            wait("t", s_s, TK["evac_s"])
            wait("t", s_v, TK["evac_v"])
        mm(ps_ssm[0][:, 0:257], stg32[:, 0:128], stg32[:, 128:385], True, True)
        mm(ps_ssm[1][:, 0:129], stg32[:, 257:385], stg32[:, 385:514], True, True)
        ev_gmm = TK["gmm"] = mm(ps_ssm[1][:, 129:258], stg32[:, 514:642],
                                stg32[:, 642:771], True, True, inc=s_t)
        evmap = ((0, ps_ssm[0][:, 0:128]), (384, ps_ssm[0][:, 128:129]),
                 (128, ps_ssm[1][:, 0:128]), (385, ps_ssm[1][:, 128:129]),
                 (256, ps_ssm[1][:, 129:257]), (386, ps_ssm[1][:, 257:258]))
        wait("s", s_t, ev_gmm)
        for dc, pslc in evmap:
            w_ = pslc.shape[1]
            op("s", lambda E, n=n, dc=dc, p=pslc, w_=w_: E.copy(gh[:, n, dc:dc + w_], p),
               outs=[gh])
        TK["evac_s"] = inc("s", s_s)
        wait("v", s_t, ev_gmm)
        wait("v", s_s, TK["evac_s"])
        for dc, pslc in evmap:
            w_ = pslc.shape[1]
            op("v", lambda E, n=n, dc=dc, p=pslc, w_=w_:
               E.scalar_tensor_tensor(gl[:, n, dc:dc + w_], p, 1.0, gh[:, n, dc:dc + w_],
                                      ALU.mult, ALU.subtract), ins=[gh], outs=[gl])
        op("v", lambda E, n=n: E.tensor_copy(m_view[:, n, :], ps_ssm[0][:, 129:257]),
           outs=[u0])
        TK["evac_v"] = inc("v", s_v)
        return None

    def wo_task(c):
        if TK["wo"] is None:
            TK["wo"] = (None, TK["split"][1])
            TK["gmm_final"] = TK["gmm"]
        wait("y", s_v, TK["wo"][1])
        wait("y", s_t, TK["gmm_final"])
        dma("y", wstage[:, :], w_hbm["wo"][c * 128:(c + 1) * 128, :], s_w)
        if TK["bias_t"] is not None:
            wait("y", s_t, TK["bias_t"])
        else:
            wait("y", s_t, TK["gmm_final"])
        dma("y", stg32b2[:, :], w_hbm["wo"][c * 128:(c + 1) * 128, :], s_w)
        ev_ld = b.cnt[s_w]
        wait("v", s_w, ev_ld)
        op("v", lambda E, c=c: E.tensor_scalar(wstage[:, :], wstage[:, :], gm[:, c:c + 1],
                                               None, ALU.mult), ins=[wstage, gm], outs=[wstage])
        ev_sc = inc("v", s_v)
        wait("s", s_v, ev_sc)
        op("s", lambda E, c=c: E.copy(woh[:, c, :], wstage[:, :]), ins=[wstage], outs=[woh])
        ev_wh = inc("s", s_s)
        wait("v", s_s, ev_wh)
        op("v", lambda E, c=c: E.scalar_tensor_tensor(wol[:, c, :], wstage[:, :], 1.0,
                                                      woh[:, c, :], ALU.mult, ALU.subtract),
           ins=[wstage, woh], outs=[wol])
        TK["wo"] = (ev_wh, inc("v", s_v))
        return lambda: wo_back(c, ev_ld)

    def wo_back(c, ev_ld):
        wait("t", s_w, ev_ld)
        mm(ps_h[0:1, H0], bm32[:, c:c + 1], stg32b2[:, H0],
           start=(c == 0), stop=(c == NBLK - 1))
        TK["bias_t"] = mm(ps_h[0:1, H1], bm32[:, c:c + 1], stg32b2[:, H1],
                          start=(c == 0), stop=(c == NBLK - 1), inc=s_t)
        return None

    def bias_evac_task():
        TK["f"] = TK["wo"]

        def back():
            wait("v", s_t, TK["bias_t"])
            op("v", lambda E: E.tensor_copy(bao_row[:, :], ps_h[0:1, :]), outs=[bao_row])
            ev_bs = inc("v", s_v)
            wait("g", s_v, ev_bs)
            dma("g", scrb[:, :].opt(), bao_row[:, :], s_c, outs=[])
            dma("g", bao_rep[:, :], scrb.ap()[0:1, :].partition_broadcast(128), s_c)
            TK["bao"] = b.cnt[s_c]
        return back

    fst_p = [wstage, fstage]

    def ffn_task(j, c):
        k = j * NBLK + c
        wst = fst_p[k % 2]
        prev = TK["f"] if k == 0 else TK["fp"][k % 2]
        if prev is not None:
            wait("y", s_s, prev[0])
            wait("y", s_v, prev[1])
        dma("y", wst[:, :], w_hbm[f"f{j + 1}"][c * 128:(c + 1) * 128, :], s_w)
        ev_ld = b.cnt[s_w]
        wait("s", s_w, ev_ld)
        split_s(fwh[j][:, c, :], wst[:, :])
        ev_fh = inc("s", s_s)

        def back():
            wait("v", s_w, ev_ld)
            wait("v", s_s, ev_fh)
            split_v(fwl[j][:, c, :], wst[:, :], fwh[j][:, c, :])
            TK["fp"][k % 2] = (ev_fh, inc("v", s_v))
        return back

    TASKS = ([(lambda n=n: g_task(n)) for n in range(NBLK)]
             + [(lambda c=c: wo_task(c)) for c in range(NBLK)]
             + [bias_evac_task]
             + [(lambda j=j, c=c: ffn_task(j, c)) for j in range(3) for c in range(NBLK)])
    MAXLEAD = [1, 1] + [2] * (NBLK - 2) + [1] * NBLK + [2] + [2] * (3 * NBLK)

    p1_load(0)
    p1_load(1)
    p1_preamble(0)
    p1_preamble(1)
    _ntask = 0
    _nback = 0
    _backs = []
    for i in range(n_tiles):
        p1_gram(i)
        if i + 2 < n_tiles:
            p1_load(i + 2)
            p1_preamble(i + 2)
        tgt_b = i * len(TASKS) // n_tiles
        tgt_f = (i + 1) * len(TASKS) // n_tiles
        while _nback < min(_ntask, tgt_b):
            f = _backs[_nback]
            if f is not None:
                f()
            _nback += 1
        while _ntask < tgt_f:
            while _nback < _ntask + 1 - MAXLEAD[_ntask]:
                f = _backs[_nback]
                if f is not None:
                    f()
                _nback += 1
            _backs.append(TASKS[_ntask]())
            _ntask += 1
    while _nback < len(TASKS):
        f = _backs[_nback]
        if f is not None:
            f()
        _nback += 1



    xfh = xnh_fm[:, :].rearrange("p (n d) -> p n d", n=NBLK)
    xfl = xnl_fm[:, :].rearrange("p (n d) -> p n d", n=NBLK)
    nfh = nfm_h.rearrange("p (n d) -> p n d", n=NBLK)
    nfl = nfm_l.rearrange("p (n d) -> p n d", n=NBLK)
    h1h = h1fm_h.rearrange("p (n d) -> p n d", n=NBLK)
    h1l = h1fm_l.rearrange("p (n d) -> p n d", n=NBLK)
    h2h = h2fm_h.rearrange("p (n d) -> p n d", n=NBLK)
    h2l = h2fm_l.rearrange("p (n d) -> p n d", n=NBLK)

    st = {"psh_rd": None,
          "psh_rd0": None,
          "out_rd": None,
          "store": [0] * (n_tiles + 1),
          "xfree": [None] * (n_tiles + 2)}
    ev_p2_xn = [None] * (n_tiles + 1)

    def p2_load(i):
        if i >= n_tiles:
            return
        if i >= 2:
            wait("y", s_v, st["xfree"][i - 2])
        else:
            wait("y", s_s, ev_p1_xrd[n_tiles - 2 + i])
        dma("y", xbuf[:, i % 2, :], x_hbm[i * 128:(i + 1) * 128, :], s_x, outs=[xbuf])

    def p2_preamble(i):
        if i >= n_tiles:
            return
        wait("v", s_x, 16 * (n_tiles + i + 1))
        ln_stats(xbuf[:, i % 2, :])
        ar = arena[i % 2]
        op("v", lambda E, i=i, ar=ar: E.tensor_scalar(ar[:, :], xbuf[:, i % 2, :],
                                                      negmu[:, 0:1], rstd[:, 0:1],
                                                      ALU.add, ALU.mult),
           ins=[xbuf, negmu, rstd], outs=[ar])
        ev_p2_xn[i] = inc("v", s_v)

    ssm_slots = [(ps_ssm[0], 0, None), (ps_ssm[1], 0, None),
                 (ps_trF, 0, "trF:h0"), (ps_trF, 512, "trF:h1")]
    ssm_slots_t0 = [(ps_ssm[0], 0, None), (ps_ssm[1], 0, None),
                    (ps_out, 0, "out:h0"), (ps_out, 512, "out:h1"),
                    (ps_trF, 0, "trF:h0"), (ps_trF, 512, "trF:h1"),
                    (ps_ssm[0], 0, None), (ps_ssm[1], 0, None)]

    def grp_ssm_ffn1(i, c, spl):
        bank, off, skey = ssm_slots_t0[c] if i == 0 else ssm_slots[c % 4]
        for sem, val in spl[0]:
            wait("t", sem, val)
        if skey:
            for sem, val in slot_free.get(skey, []):
                wait("t", sem, val)
        else:
            slot_wait("t", bank)
        Xh, Xl = xfh[:, c, :], xfl[:, c, :]
        bk = bank[:, off:off + 387]
        mm(bk, Xh, gh[:, c, :], True, False)
        mm(bk, Xh, gl[:, c, :], False, False)
        for sem, val in spl[1]:
            wait("t", sem, val)
        ev_ssm = mm(bk, Xl, gh[:, c, :], False, True, inc=s_t)
        if c == 0:
            if st["psh_rd0"]:
                wait("t", s_v, st["psh_rd0"])
            mm(ps_h[:, H0], Xh, fwh[0][:, c, H0], True, False)
            mm(ps_h[:, H0], Xh, fwl[0][:, c, H0], False, False)
            mm(ps_h[:, H0], Xl, fwh[0][:, c, H0], False, False)
            if st["psh_rd"]:
                wait("t", s_v, st["psh_rd"])
            mm(ps_h[:, H1], Xh, fwh[0][:, c, H1], True, False)
            mm(ps_h[:, H1], Xh, fwl[0][:, c, H1], False, False)
            ev_grp = mm(ps_h[:, H1], Xl, fwh[0][:, c, H1], False, False, inc=s_t)
            return ev_ssm, None, ev_grp
        if c == NBLK - 1:
            mm(ps_h[:, H0], Xh, fwh[0][:, c, H0], False, False)
            mm(ps_h[:, H0], Xh, fwl[0][:, c, H0], False, False)
            ev_h0 = mm(ps_h[:, H0], Xl, fwh[0][:, c, H0], False, True, inc=s_t)
            mm(ps_h[:, H1], Xh, fwh[0][:, c, H1], False, False)
            mm(ps_h[:, H1], Xh, fwl[0][:, c, H1], False, False)
            ev_grp = mm(ps_h[:, H1], Xl, fwh[0][:, c, H1], False, True, inc=s_t)
            return ev_ssm, ev_h0, ev_grp
        mm(ps_h[:, H0], Xh, fwh[0][:, c, H0], c == 0, False)
        mm(ps_h[:, H1], Xh, fwh[0][:, c, H1], c == 0, False)
        mm(ps_h[:, H0], Xh, fwl[0][:, c, H0], False, False)
        mm(ps_h[:, H1], Xh, fwl[0][:, c, H1], False, False)
        mm(ps_h[:, H0], Xl, fwh[0][:, c, H0], False, False)
        ev_grp = mm(ps_h[:, H1], Xl, fwh[0][:, c, H1], False, False, inc=s_t)
        return ev_ssm, None, ev_grp

    def blend(i, c, ev_ssm):
        bank, off, skey = ssm_slots_t0[c] if i == 0 else ssm_slots[c % 4]
        ar = arena[i % 2]
        wait("v", s_t, ev_ssm)
        if i == 0 and c == 0:
            wait("v", s_c, ev_bc)
        op("v", lambda E, bank=bank, off=off: E.tensor_scalar(
            g3[:, :], bank[:, off + 384:off + 387], c01b[:, 0:1], omcb[:, 0:1],
            ALU.mult, ALU.add), ins=[c01b, omcb], outs=[g3])
        for t3 in range(3):
            op("v", lambda E, bank=bank, off=off, t3=t3, c=c, ar=ar:
               E.scalar_tensor_tensor(ar[:, c * 128:(c + 1) * 128],
                                      bank[:, off + t3 * 128:off + (t3 + 1) * 128],
                                      g3[:, t3:t3 + 1], ar[:, c * 128:(c + 1) * 128],
                                      ALU.mult, ALU.add),
               ins=[g3, ar], outs=[ar])
        ev = inc("v", s_v)
        slot_free[_nm(bank[:, 0:1])] = [(s_v, ev)]
        if skey:
            slot_free[skey] = [(s_v, ev)]
        return ev

    def wide_grp(stat_h, stat_l, c, wh, wl, pdst, start, stop, hi_wait=(), lo_wait=(),
                 inc_last=None, h0_first=False):
        for sem, val in hi_wait:
            wait("t", sem, val)
        Sh, Sl = stat_h[:, c, :], stat_l[:, c, :]
        if h0_first:
            mm(pdst[:, H0], Sh, wh[:, c, H0], start, False)
            mm(pdst[:, H0], Sh, wl[:, c, H0], False, False)
            for sem, val in lo_wait:
                wait("t", sem, val)
            ev_h0 = mm(pdst[:, H0], Sl, wh[:, c, H0], False, stop, inc=inc_last)
            mm(pdst[:, H1], Sh, wh[:, c, H1], start, False)
            mm(pdst[:, H1], Sh, wl[:, c, H1], False, False)
            return ev_h0, mm(pdst[:, H1], Sl, wh[:, c, H1], False, stop, inc=inc_last)
        mm(pdst[:, H0], Sh, wh[:, c, H0], start, False)
        mm(pdst[:, H1], Sh, wh[:, c, H1], start, False)
        mm(pdst[:, H0], Sh, wl[:, c, H0], False, False)
        mm(pdst[:, H1], Sh, wl[:, c, H1], False, False)
        for sem, val in lo_wait:
            wait("t", sem, val)
        ev_h0 = mm(pdst[:, H0], Sl, wh[:, c, H0], False, stop, inc=inc_last)
        return ev_h0, mm(pdst[:, H1], Sl, wh[:, c, H1], False, stop, inc=inc_last)


    wait("v", s_t, ev_p1_gram[n_tiles - 1])
    op("v", lambda E: E.tensor_tensor(outbuf[:, :], ps_out[:, :], u0[:, :], ALU.mult),
       ins=[u0, outbuf], outs=[outbuf])
    op("v", lambda E: E.reduce_sum(negmu[:, 0:1], outbuf[:, :], axis=AX.X),
       ins=[outbuf], outs=[negmu])
    ev_red = inc("v", s_v)
    slot_free["out:h0"] = [(s_v, ev_red)]
    slot_free["out:h1"] = [(s_v, ev_red)]
    wait("t", s_v, ev_red)
    op("t", lambda E: E.transpose(ps_trF[0:1, 0:128], negmu[:, 0:1], eye32[:, :]))
    ev_tt = inc("t", s_t)
    wait("v", s_t, ev_tt)
    op("v", lambda E: E.reduce_sum(sca[0:1, 1:2], ps_trF[0:1, 0:128], axis=AX.X),
       ins=[sca], outs=[sca])
    op("v", lambda E: E.tensor_scalar(sca[0:1, 2:3], sca[0:1, 1:2], 1.0 / float(S_loc),
                                      None, ALU.mult), ins=[sca], outs=[sca])
    ev_v1 = inc("v", s_v)
    wait("s", s_v, ev_v1)
    op("s", lambda E: E.activation(sca[0:1, 3:4], sca[0:1, 0:1], ACT.Ln), ins=[sca], outs=[sca])
    ev_s1 = inc("s", s_s)
    wait("v", s_s, ev_s1)
    op("v", lambda E: E.reciprocal(sca[0:1, 4:5], sca[0:1, 3:4]), ins=[sca], outs=[sca])
    op("v", lambda E: E.tensor_scalar(sca[0:1, 5:6], sca[0:1, 2:3], 10.0, None, ALU.mult),
       ins=[sca], outs=[sca])
    op("v", lambda E: E.tensor_tensor(sca[0:1, 6:7], sca[0:1, 5:6], sca[0:1, 4:5], ALU.mult),
       ins=[sca], outs=[sca])
    op("v", lambda E: E.tensor_scalar(sca[0:1, 7:8], sca[0:1, 6:7], -1.0, CLIP,
                                      ALU.mult, ALU.min), ins=[sca], outs=[sca])
    op("v", lambda E: E.tensor_scalar(sca[0:1, 7:8], sca[0:1, 7:8], -CLIP, None, ALU.max),
       ins=[sca], outs=[sca])
    ev_v2 = inc("v", s_v)
    wait("s", s_v, ev_v2)
    op("s", lambda E: E.activation(sca[0:1, 8:9], sca[0:1, 7:8], ACT.Exp), ins=[sca], outs=[sca])
    ev_s2 = inc("s", s_s)
    wait("v", s_s, ev_s2)
    op("v", lambda E: E.tensor_scalar(sca[0:1, 9:10], sca[0:1, 8:9], 1.0, None, ALU.add),
       ins=[sca], outs=[sca])
    op("v", lambda E: E.reciprocal(sca[0:1, 10:11], sca[0:1, 9:10]), ins=[sca], outs=[sca])
    op("v", lambda E: E.tensor_scalar(ccv[:, :], oh_sb[:, :], sca[0:1, 10:11], None, ALU.mult),
       ins=[oh_sb, sca], outs=[ccv])
    ev_ccv = inc("v", s_v)
    wait("g", s_v, ev_ccv)
    dma("g", cc_in[:, :].opt(), ccv[:, :], s_c, outs=[])
    b.wait("g", s_c, b.cnt[s_c])
    op("g", lambda E: E.collective_compute("AllReduce", ALU.add,
                                           replica_groups=[list(range(B))],
                                           ins=[cc_in.ap().opt()], outs=[cc_out.ap().opt()]),
       inc=s_cc, amt=1)
    b.wait("g", s_cc, b.cnt[s_cc])
    dma("g", alldwc[:, :], cc_out[:, :].opt(), s_c)
    ev_adwc = b.cnt[s_c]
    p2_load(0)
    p2_load(1)
    p2_preamble(0)
    halves_nxt = tr_phase(arena[0][:, :], xnh_fm[:, :], xnl_fm[:, :],
                          src_wait=[(s_v, ev_p2_xn[0])], hi_eng="v")

    wait("v", s_c, ev_adwc)
    op("v", lambda E: E.reduce_sum(sca[0:1, 11:12], alldwc[:, :], axis=AX.X),
       ins=[alldwc, sca], outs=[sca])
    op("v", lambda E: E.tensor_scalar(sca[0:1, 12:13], sca[0:1, 11:12], 0.125, None, ALU.mult),
       ins=[sca], outs=[sca])
    op("v", lambda E: E.tensor_scalar(sca[0:1, 13:14], sca[0:1, 12:13], 0.99, None, ALU.is_gt),
       ins=[sca], outs=[sca])
    op("v", lambda E: E.tensor_scalar(sca[0:1, 14:15], sca[0:1, 12:13], 0.01, None, ALU.is_lt),
       ins=[sca], outs=[sca])
    op("v", lambda E: E.tensor_scalar(sca[0:1, 15:16], sca[0:1, 13:14], -1.0, 1.0,
                                      ALU.mult, ALU.add), ins=[sca], outs=[sca])
    op("v", lambda E: E.tensor_scalar(sca[0:1, 16:17], sca[0:1, 14:15], -1.0, 1.0,
                                      ALU.mult, ALU.add), ins=[sca], outs=[sca])
    op("v", lambda E: E.tensor_tensor(sca[0:1, 17:18], sca[0:1, 15:16], sca[0:1, 16:17],
                                      ALU.mult), ins=[sca], outs=[sca])
    op("v", lambda E: E.tensor_tensor(sca[0:1, 18:19], sca[0:1, 17:18], sca[0:1, 10:11],
                                      ALU.mult), ins=[sca], outs=[sca])
    op("v", lambda E: E.tensor_tensor(sca[0:1, 19:20], sca[0:1, 18:19], sca[0:1, 13:14],
                                      ALU.add), ins=[sca], outs=[sca])
    op("v", lambda E: E.tensor_scalar(sca[0:1, 20:21], sca[0:1, 19:20], 0.1, None, ALU.mult),
       ins=[sca], outs=[sca])
    op("v", lambda E: E.tensor_scalar(sca[0:1, 21:22], sca[0:1, 19:20], -1.0, 1.0,
                                      ALU.mult, ALU.add), ins=[sca], outs=[sca])
    ev_coef = inc("v", s_v)
    st["out_rd"] = ev_coef
    wait("g", s_v, ev_coef)
    dma("g", scr2[:, :].opt(), sca[0:1, 20:22], s_c, outs=[])
    b.wait("g", s_c, b.cnt[s_c])
    dma("g", com2[:, :], scr2.ap()[0:1, 0:2].partition_broadcast(128), s_c)
    ev_bc = b.cnt[s_c]

    def tail_post(p):
        i0, ar0 = p["i"], p["ar"]
        wait("v", s_s, p["ev_exp"])
        wait("v", s_t, p["ev_wo_t"])
        if i0 == 0:
            wait("v", s_c, TK["bao"])
        op("v", lambda E, i0=i0: E.scalar_tensor_tensor(outbuf[:, :], ps_out[:, :], 1.0,
                                                        xbuf[:, i0 % 2, :], ALU.mult, ALU.add),
           ins=[xbuf, outbuf], outs=[outbuf])
        op("v", lambda E, ar0=ar0: E.tensor_tensor(outbuf[:, :], outbuf[:, :], ar0[:, :],
                                                   ALU.add), ins=[outbuf, ar0], outs=[outbuf])
        op("v", lambda E: E.tensor_tensor(outbuf[:, :], outbuf[:, :], bao_rep[:, :],
                                          ALU.add), ins=[outbuf, bao_rep], outs=[outbuf])
        ev_fin = inc("v", s_v)
        st["out_rd"] = ev_fin
        st["xfree"][i0] = ev_fin
        wait("g", s_v, ev_fin)
        dma("g", out_hbm[i0 * 128:(i0 + 1) * 128, :], outbuf[:, :], s_o, outs=[])
        st["store"][i0] = b.cnt[s_o]
        p2_load(i0 + 2)

    prev = None
    for i in range(n_tiles):
        ar = arena[i % 2]
        halves = halves_nxt
        for c in range(NBLK):
            ev_ssm, ev_f1h0, ev_grp = grp_ssm_ffn1(i, c, halves[c // 4])
            blend(i, c, ev_ssm)
        ev_ffn1 = b.cnt[s_t]

        wait("s", s_t, ev_f1h0)
        op("s", lambda E: E.copy(u0[:, 0:256], ps_h[:, 0:256]), ins=[u0], outs=[u0])
        ev_h32p = inc("s", s_s)
        op("s", lambda E: E.copy(u0[:, 256:512], ps_h[:, 256:512]), ins=[u0], outs=[u0])
        ev_h32a = inc("s", s_s)
        wait("s", s_t, ev_ffn1)
        op("s", lambda E: E.copy(u0[:, 512:1024], ps_h[:, 512:1024]), ins=[u0], outs=[u0])
        ev_h32 = inc("s", s_s)
        h1_halves = tr_phase(u0[:, :], h1fm_h, h1fm_l, src_wait=[(s_s, ev_h32p)],
                             src_wait_mid=[(s_s, ev_h32a)],
                             src_wait2=[(s_s, ev_h32)])

        for c in range(NBLK - 1):
            wide_grp(h1h, h1l, c, fwh[1], fwl[1], ps_h, c == 0, False,
                     hi_wait=h1_halves[c // 4][0], lo_wait=h1_halves[c // 4][1])

        if prev is not None:
            tail_post(prev)

        ln_stats(ar)
        op("v", lambda E, ar=ar: E.tensor_scalar(ar[:, :], ar[:, :], negmu[:, 0:1],
                                                 rstd[:, 0:1], ALU.add, ALU.mult),
           ins=[ar, negmu, rstd], outs=[ar])
        ev_norm = inc("v", s_v)

        n_halves = tr_phase(ar[:, :], nfm_h, nfm_l, src_wait=[(s_v, ev_norm)])
        ev_ffn2 = wide_grp(h1h, h1l, NBLK - 1, fwh[1], fwl[1], ps_h, False, True,
                           hi_wait=h1_halves[1][0], lo_wait=h1_halves[1][1],
                           inc_last=s_t, h0_first=True)
        wait("s", s_t, ev_ffn2[0])
        op("s", lambda E: E.copy(u0[:, 0:512], ps_h[:, 0:512]), ins=[u0], outs=[u0])
        ev_h32c = inc("s", s_s)
        wait("s", s_t, ev_ffn2[1])
        op("s", lambda E: E.copy(u0[:, 512:1024], ps_h[:, 512:1024]), ins=[u0], outs=[u0])
        ev_h32b = inc("s", s_s)

        p2_preamble(i + 1)

        for c in range(NBLK - 1):
            ew = list(n_halves[c // 4][0])
            if c == 0:
                ew.append((s_v, st["out_rd"]))
                ew += slot_free.get(_nm(ps_out[:, 0:1]), [])
            wide_grp(nfh, nfl, c, woh, wol, ps_out, c == 0, False,
                     hi_wait=ew, lo_wait=n_halves[c // 4][1])

        h2_halves = tr_phase(u0[:, :], h2fm_h, h2fm_l, src_wait=[(s_s, ev_h32c)],
                             src_wait2=[(s_s, ev_h32b)])
        ev_wo_t = wide_grp(nfh, nfl, NBLK - 1, woh, wol, ps_out, False, True,
                           hi_wait=n_halves[1][0], lo_wait=n_halves[1][1],
                           inc_last=s_t)[1]

        for c in range(2):
            wide_grp(h2h, h2l, c, fwh[2], fwl[2], ps_h, c == 0, False,
                     hi_wait=h2_halves[c // 4][0], lo_wait=h2_halves[c // 4][1])
        if i + 1 < n_tiles:
            halves_nxt = tr_phase(arena[(i + 1) % 2][:, :], xnh_fm[:, :], xnl_fm[:, :],
                                  src_wait=[(s_v, ev_p2_xn[i + 1])], hi_eng="v")
        for c in range(2, NBLK):
            ev_ffn3 = wide_grp(h2h, h2l, c, fwh[2], fwl[2], ps_h, False, c == NBLK - 1,
                               hi_wait=h2_halves[c // 4][0], lo_wait=h2_halves[c // 4][1],
                               inc_last=s_t if c == NBLK - 1 else None,
                               h0_first=(c == NBLK - 1))

        op("s", lambda E: E.activation(sca[0:1, 30:31], sca[0:1, 30:31], ACT.Tanh),
           ins=[sca], outs=[sca])
        wait("v", s_t, ev_ffn3[0])
        if i == 0:
            wait("v", s_w, ev_w0)
        wait("v", s_o, st["store"][i - 1] if i >= 1 else 0)
        op("v", lambda E: E.tensor_tensor(outbuf[:, H0], ps_h[:, H0], fb_rep[:, H0],
                                          ALU.add), ins=[fb_rep, outbuf], outs=[outbuf])
        ev_fb0 = inc("v", s_v)
        st["psh_rd0"] = ev_fb0
        wait("v", s_t, ev_ffn3[1])
        op("v", lambda E: E.tensor_tensor(outbuf[:, H1], ps_h[:, H1], fb_rep[:, H1],
                                          ALU.add), ins=[fb_rep, outbuf], outs=[outbuf])
        ev_fb = inc("v", s_v)
        st["psh_rd"] = ev_fb
        wait("s", s_v, ev_fb)
        op("s", lambda E, ar=ar: E.activation(ar[:, :], outbuf[:, :], ACT.Tanh),
           ins=[outbuf], outs=[ar])
        ev_exp = inc("s", s_s)
        op("s", lambda E: E.activation(sca[0:1, 31:32], epst[0:1, 0:1], ACT.Ln,
                                       bias=epst[0:1, 0:1]), ins=[sca, epst], outs=[sca])
        prev = {"i": i, "ar": ar, "ev_wo_t": ev_wo_t, "ev_exp": ev_exp}

    tail_post(prev)
    b.wait("g", s_o, b.cnt[s_o])

    block = es.enter_context(nc.Block())
    with es:
        @block.sync
        def _(E):
            for f in b.SCH["y"]:
                f(E)

        @block.vector
        def _(E):
            for f in b.SCH["v"]:
                f(E)

        @block.scalar
        def _(E):
            for f in b.SCH["s"]:
                f(E)

        @block.tensor
        def _(E):
            for f in b.SCH["t"]:
                f(E)

        @block.gpsimd
        def _(E):
            for f in b.SCH["g"]:
                f(E)
    return nc


_CACHE = {}


def _get_nc(n_tiles):
    if n_tiles not in _CACHE:
        _CACHE[n_tiles] = build_nc(n_tiles)
    return _CACHE[n_tiles]


def kernel(L0_attn_x, L0_attn_W_q, L0_attn_W_k, L0_attn_W_v, L0_attn_W_o,
           L0_attn_SSM_A, L0_attn_SSM_B, L0_attn_SSM_C, L0_attn_ln_gamma,
           L0_attn_ln_beta, L0_ffn_weight, L0_ffn_aux_weight1, L0_ffn_aux_weight2,
           L0_ffn_bias):
    x = np.ascontiguousarray(np.asarray(L0_attn_x, dtype=np.float32))
    Bsz, S_, D_ = x.shape
    assert D_ == D and Bsz == B
    n_tiles = S_ // 128
    idx = np.arange(NBLK)

    def diag(M):
        M = np.asarray(M, dtype=np.float32)
        return np.ascontiguousarray(M.reshape(NBLK, 128, NBLK, 128)[idx, :, idx, :])

    common = {
        "wq": np.ascontiguousarray(np.asarray(L0_attn_W_q, dtype=np.float32)),
        "wk": np.ascontiguousarray(np.asarray(L0_attn_W_k, dtype=np.float32)),
        "wv": np.ascontiguousarray(np.asarray(L0_attn_W_v, dtype=np.float32)),
        "ad": diag(L0_attn_SSM_A),
        "bd": diag(L0_attn_SSM_B),
        "cd": diag(L0_attn_SSM_C),
        "wo": np.ascontiguousarray(np.asarray(L0_attn_W_o, dtype=np.float32)),
        "gamma": np.ascontiguousarray(np.asarray(L0_attn_ln_gamma, dtype=np.float32)),
        "beta": np.ascontiguousarray(np.asarray(L0_attn_ln_beta, dtype=np.float32)),
        "f1": np.ascontiguousarray(np.asarray(L0_ffn_weight, dtype=np.float32)),
        "f2": np.ascontiguousarray(np.asarray(L0_ffn_aux_weight1, dtype=np.float32)),
        "f3": np.ascontiguousarray(np.asarray(L0_ffn_aux_weight2, dtype=np.float32)),
        "fb": np.ascontiguousarray(np.asarray(L0_ffn_bias, dtype=np.float32)),
        "eye": np.eye(128, dtype=np.float32),
    }
    eyeB = np.eye(B, dtype=np.float32)
    in_maps = [dict(common, x=np.ascontiguousarray(x[bb]),
                    onehot=np.ascontiguousarray(eyeB[bb:bb + 1]))
               for bb in range(B)]
    nc = _get_nc(n_tiles)
    res = run_bass_kernel_spmd(nc, in_maps, core_ids=list(range(B)))
    return np.stack([np.asarray(r["out"]) for r in res.results], axis=0)
